# version 19
# baseline (speedup 1.0000x reference)
import contextlib
import numpy as np
import ml_dtypes
import concourse.bass as bass
import concourse.mybir as mybir
import concourse.dve_ops as dve_ops
from concourse.dve_op import DveOp
from concourse.dve_spec import (
    Spec, Src0, C0, C1, C2, C3, Zero, select, maxx, lower as dve_lower,
)
from concourse.dve_table_gen import dve_ver_for

F32 = mybir.dt.float32
BF16 = mybir.dt.bfloat16
OP = mybir.AluOpType
AF = mybir.ActivationFunctionType
AX = mybir.AxisListType
BF = ml_dtypes.bfloat16

C, H, Dh = 1024, 16, 64
NPAIR, NG = 8, 2
PPG = NPAIR // NG


def _register_custom_dve():
    if any(o.name == "QINVS6" for o in dve_ops.OPS):
        return
    ver = dve_ver_for("TRN2")
    a = maxx(Src0, Zero - Src0)
    c1 = a >= C0
    c2 = a >= C1
    spec_q = Spec(body=(C1 + C1) - (c1 * C1) - (c2 * C0))

    def ref_q(in0, in1, s0, s1, imm2):
        aa = np.abs(in0)
        cc1 = (aa >= s0).astype(in0.dtype)
        cc2 = (aa >= s1).astype(in0.dtype)
        return (s1 + s1) - cc1 * s1 - cc2 * s0

    u1 = select(Src0 >= C0, Src0 - C1, Src0)
    spec_l = Spec(body=select(u1 >= C2, u1 - C3, u1))

    def ref_l(in0, in1, s0, s1, imm2):
        w2 = np.asarray(in1)
        uu1 = np.where(in0 >= s0, in0 - s1, in0)
        return np.where(uu1 >= imm2, uu1 - w2, uu1)

    for name, spec, ref in (("QINVS6", spec_q, ref_q), ("LADDER2", spec_l, ref_l)):
        lowered = dve_lower(spec, ver=ver, name=name)
        op = DveOp(name=name, spec=spec, subdim=False,
                   uops_sha={ver: lowered.sha(ver)}, reference=ref)
        dve_ops.OPS.append(op)
        dve_ops._SUB_OPCODE_FOR_NAME[name] = (
            dve_ops._CUSTOM_DVE_ROW_BASE + len(dve_ops.OPS) - 1)
        dve_ops.CUSTOM_DVE_SPECS[name] = spec
        dve_ops.CUSTOM_DVE_REFS[name] = ref


_register_custom_dve()
OP_QINVS6 = next(o for o in dve_ops.OPS if o.name == "QINVS6")
OP_LADDER2 = next(o for o in dve_ops.OPS if o.name == "LADDER2")


class Plan:
    ENGINES = ("sync", "tensor", "vector", "scalar", "gpsimd")
    def __init__(self, nc):
        self.nc = nc
        self.steps = []
        self.sems = {}
        self.cnt = {}
        self.kbar = 0
        self.dma_sems = {}
        for s in ("MM", "ACT", "VE", "GPC", "PH"):
            self.sem(s)
    def sem(self, name):
        if name not in self.sems:
            self.sems[name] = self.nc._ctx.enter_context(self.nc.semaphore(name))
            self.cnt[name] = 0
        return self.sems[name]
    def tok(self, name):
        self.sem(name)
        return self.cnt[name]
    def emit(self, engine, fn, wait=(), inc=()):
        for s, _ in wait:
            self.sem(s)
        wait = [(s, v) for (s, v) in wait if v > 0]
        for s, n in inc:
            self.sem(s)
            self.cnt[s] += n
            if n == 16:
                self.dma_sems.setdefault(engine, set()).add(s)
        self.steps.append((engine, fn, tuple(wait), tuple(inc)))
    def barrier_all(self):
        self.kbar += 1
        for en in self.ENGINES:
            own = [(s, self.cnt[s]) for s in sorted(self.dma_sems.get(en, ()))
                   if self.cnt[s] > 0]
            self.cnt["PH"] += 1
            self.steps.append((en, ("PHINC",), tuple(own), ()))
        tgt = 5 * self.kbar
        for en in self.ENGINES:
            self.steps.append((en, lambda e: None, (("PH", tgt),), ()))
    def replay(self, engine, eng):
        for e, fn, waits, incs in self.steps:
            if e != engine:
                continue
            if isinstance(fn, tuple) and fn[0] == "PHINC":
                for s, v in waits:
                    eng.wait_ge(self.sems[s], v)
                eng.drain()
                eng.sem_inc(self.sems["PH"], 1)
                continue
            for s, v in waits:
                eng.wait_ge(self.sems[s], v)
            ins = fn(eng)
            for s, n in incs:
                assert ins is not None
                ins = ins.then_inc(self.sems[s], n)


def host_constants(T, D=64):
    inv_freq = 1.0 / (10000.0 ** (np.arange(0, D, 2, dtype=np.float64) / D))
    pos = np.arange(T, dtype=np.float64)
    freqs = pos[:, None] * inv_freq[None, :]
    emb = np.concatenate([freqs, freqs], axis=-1)
    sin = np.sin(emb).astype(np.float32)
    cos = np.cos(emb).astype(np.float32)
    cosT = np.tile(cos.T, (2, 1)).astype(np.float32)
    sgn = np.where(np.arange(64) < 32, -1.0, 1.0).astype(np.float32)
    sinT = np.tile((sin.T * sgn[:, None]), (2, 1)).astype(np.float32)
    permT = np.zeros((128, 128), dtype=np.float32)
    for blk in range(2):
        for j in range(64):
            s = 2 * j + 1 if j < 32 else 2 * (j - 32)
            permT[blk * 64 + s, blk * 64 + j] = 1.0
    iden = np.eye(128, dtype=np.float32)
    idenb = np.eye(128, dtype=BF)
    ones64 = np.ones((128, 64), dtype=BF)
    ones1 = np.ones((1, 128), dtype=BF)
    return dict(cosT=cosT, sinT=sinT, permT=permT, iden=iden, idenb=idenb,
                ones64=ones64, ones1=ones1)


def host_inputs_for_core(x_core, Wqkv, bqkv, Wout, bout, Wffn, bffn, T):
    cst = host_constants(T)
    bqkv_t = np.zeros((128, 24), dtype=np.float32)
    for oc in range(24):
        bqkv_t[:, oc] = bqkv[oc * 128:(oc + 1) * 128]
    src = np.array([2 * j + 1 if j < 32 else 2 * (j - 32) for j in range(64)])
    srcs = np.concatenate([src, src + 64])
    bqkv_sh = np.zeros((128, 16), dtype=np.float32)
    for oc in range(16):
        col = bqkv_t[:, oc]
        bqkv_sh[:, oc] = col[srcs]
    brow = np.zeros((1, 3 * C), dtype=np.float32)
    brow[0, 0:C] = bqkv[2 * C:3 * C]
    brow[0, C:2 * C] = bffn
    brow[0, 2 * C:3 * C] = bout
    return {
        "x": np.ascontiguousarray(x_core),
        "wqkv": np.ascontiguousarray(Wqkv), "wout": np.ascontiguousarray(Wout),
        "wffn": np.ascontiguousarray(Wffn),
        "i_bqkv_t": bqkv_t, "i_bqkv_sh": bqkv_sh, "brow": brow,
        "i_cosT": cst["cosT"], "i_sinT": cst["sinT"], "i_permT": cst["permT"],
        "i_iden": cst["iden"], "i_idenb": cst["idenb"],
        "i_ones64": cst["ones64"], "i_ones1": cst["ones1"],
        "i_qthr": np.tile(np.array([[-2.0, -4.0, -31.5, 0.0, 16.0, 4.0, 1.0]],
                                   np.float32), (128, 1)),
    }


GP_LADDER = False


def build_core_program(T=2048, Bb=2):
    Tt = min(T, 512)
    NTT = T // Tt
    NSC = T // 128
    NTB = T // 128

    nc = bass.Bass("TRN2", target_bir_lowering=False)
    nc._ctx = contextlib.ExitStack()

    def din(name, shape, dt=F32):
        return nc.dram_tensor(name, shape, dt, kind="ExternalInput").ap()
    def dout(name, shape, dt=F32):
        return nc.dram_tensor(name, shape, dt, kind="ExternalOutput").ap()
    def dtmp(name, shape, dt):
        return nc.dram_tensor(name, shape, dt, kind="Internal").ap()

    X = din("x", [Bb, T, C])
    WQKV = din("wqkv", [C, 3 * C])
    WOUT = din("wout", [C, C])
    WFFN = din("wffn", [C, C])
    BQKV_T = din("i_bqkv_t", [128, 24])
    BQKV_SH = din("i_bqkv_sh", [128, 16])
    BROW = din("brow", [1, 3 * C])
    COS = din("i_cosT", [128, T])
    SIN = din("i_sinT", [128, T])
    PERM = din("i_permT", [128, 128])
    IDEN = din("i_iden", [128, 128])
    IDENB = din("i_idenb", [128, 128], BF16)
    ONES64 = din("i_ones64", [128, 64], BF16)
    ONES1 = din("i_ones1", [1, 128], BF16)
    QTHR = din("i_qthr", [128, 7])
    OUTX = dout("out", [Bb, T, C])

    QS_H = dtmp("qs_h", [Bb, NPAIR, 128, T], BF16)
    QS_L = dtmp("qs_l", [Bb, NPAIR, 128, T], BF16)
    KS_H = dtmp("ks_h", [Bb, NPAIR, 128, T], BF16)
    KS_L = dtmp("ks_l", [Bb, NPAIR, 128, T], BF16)
    VS_H = dtmp("vs_h", [Bb, NSC, 128, C], BF16)
    VS_L = dtmp("vs_l", [Bb, NSC, 128, C], BF16)
    AT_H = dtmp("at_h", [Bb, NPAIR, 128, T], BF16)
    AT_L = dtmp("at_l", [Bb, NPAIR, 128, T], BF16)

    cosT = nc.alloc_sbuf_tensor("cosT", [128, T], F32)
    sinT = nc.alloc_sbuf_tensor("sinT", [128, T], F32)
    permT = nc.alloc_sbuf_tensor("permT", [128, 128], F32)
    iden = nc.alloc_sbuf_tensor("iden", [128, 128], F32)
    idenb = nc.alloc_sbuf_tensor("idenb", [128, 128], BF16)
    ones64 = nc.alloc_sbuf_tensor("ones64", [128, 64], BF16)
    ones1 = nc.alloc_sbuf_tensor("ones1", [1, 128], BF16)
    bqkv_t = nc.alloc_sbuf_tensor("bqkv_t", [128, 24], F32)
    bqkv_sh = nc.alloc_sbuf_tensor("bqkv_sh", [128, 16], F32)
    qthr = nc.alloc_sbuf_tensor("qthr", [128, 7], F32)

    NMAIN = 48648
    main = nc.alloc_sbuf_tensor("main", [128, NMAIN], F32)

    class _Row:
        def __init__(self, off_b):
            self.off_b = off_b
        def ap(self):
            return main.ap().bitcast(BF16)[0:1, self.off_b // 2: self.off_b // 2 + 3 * C]
    o_browh = 4 * NMAIN - 2 * 3 * C * 2
    o_browl = o_browh + 3 * C * 2
    brow_h = _Row(o_browh)
    brow_l = _Row(o_browl)

    def f32(off_b, cols):
        assert off_b % 4 == 0 and off_b + 4 * cols <= 4 * NMAIN, (off_b, cols)
        return main.ap()[:, off_b // 4: off_b // 4 + cols]
    def bf16v(off_b, cols):
        assert off_b % 2 == 0 and off_b + 2 * cols <= 4 * NMAIN, (off_b, cols)
        return main.ap().bitcast(BF16)[:, off_b // 2: off_b // 2 + cols]

    o_xth, o_xtl = 0, 8 * T * 2
    o_wwh = o_xtl + 8 * T * 2
    o_wwl = o_wwh + 8 * C * 2
    o_wstg = o_wwl + 8 * C * 2
    o_xload = o_wstg + 2 * C * 4
    o_rt1 = o_xload + 2 * C * 4
    o_rt2 = o_rt1 + 2 * Tt * 4
    o_rt3 = o_rt2 + 2 * Tt * 4
    o_sth = o_rt3 + 2 * Tt * 4
    o_stl = o_sth + 2 * Tt * 2
    o_vth = o_stl + 2 * Tt * 2
    o_vtl = o_vth + 2 * C * 2
    o_rtq = o_vtl + 2 * C * 2
    p1_end = o_rtq + 2 * Tt * 4
    o_kgh, o_kgl = 0, PPG * T * 2
    o_vgh = o_kgl + PPG * T * 2
    o_vgl = o_vgh + NSC * 512 * 2
    o_qth = o_vgl + NSC * 512 * 2
    o_qtl = o_qth + PPG * Tt * 2
    o_e32 = o_qtl + PPG * Tt * 2
    o_el = o_e32 + 3 * 1024 * 4
    o_zr = o_el + 3 * 1024 * 2
    o_anf = o_zr + 2 * 512 * 4
    o_anh = o_anf + 2 * 512 * 4
    o_anl = o_anh + 2 * 512 * 2
    p2_end = o_anl + 2 * 512 * 2
    o_woh, o_wol = 0, 8 * C * 2
    o_wfh = o_wol + 8 * C * 2
    o_wfl = o_wfh + 8 * C * 2
    o_ath = o_wfl + 8 * C * 2
    o_atl = o_ath + 8 * Tt * 2
    o_x3 = o_atl + 8 * Tt * 2
    o_sres = o_x3 + 2 * C * 4
    o_x1t = o_sres + 2 * 2 * C * 4
    o_sq = o_x1t + 2 * C * 4
    o_qa = o_sq + 2 * C * 4
    o_qb = o_qa + 2 * C * 4
    o_qc = o_qb + 2 * C * 4
    o_qd = o_qc + 2 * C * 4
    o_st1 = o_qd + 2 * C * 4
    o_am = o_st1 + 16 * 4
    o_rsc = o_am + 64 * 4
    o_sc2 = o_rsc + 64 * 4
    p3_end = o_sc2 + 64 * 4
    o_xd4 = o_qb
    o_xdh = o_sq
    o_xdl = o_sq + 2 * C * 2
    o_o3 = o_qc
    o_x1t2 = o_qa
    o_wstg3 = o_x3
    o_xdtd = p3_end
    o_sres3 = o_xdtd + 2 * (2 * C * 2 + 2 * C * 2)
    assert o_sres3 + 2 * C * 4 <= o_browh, (o_sres3, o_browh)
    def sres_off(p3_):
        return o_sres + p3_ * 2 * C * 4 if p3_ < 2 else o_sres3
    assert max(p1_end, p2_end, p3_end) <= o_browh, (p1_end, p2_end, p3_end, o_browh)

    PSU = [nc._ctx.enter_context(nc.psum_tensor(f"ps{i}", [128, 1024], F32))
           for i in range(4)]

    p = Plan(nc)

    def dma(queue, sem, out_ap, in_ap, wait=()):
        p.emit(queue, lambda e, o=out_ap, i=in_ap: e.dma_start(out=o, in_=i),
               wait=wait, inc=[(sem, 16)])
        return p.tok(sem)

    def ve(fn, wait=(), drain=False):
        if drain:
            p.emit("vector", lambda e: (lambda r: (e.drain(), r)[1])(fn(e)),
                   wait=wait, inc=[("VE", 1)])
        else:
            p.emit("vector", fn, wait=wait, inc=[("VE", 1)])
        return p.tok("VE")

    def sca(fn, wait=(), drain=False):
        if drain:
            p.emit("scalar", lambda e: (lambda r: (e.drain(), r)[1])(fn(e)),
                   wait=wait, inc=[("ACT", 1)])
        else:
            p.emit("scalar", fn, wait=wait, inc=[("ACT", 1)])
        return p.tok("ACT")

    def gp(fn, wait=(), drain=False):
        if drain:
            p.emit("gpsimd", lambda e: (lambda r: (e.drain(), r)[1])(fn(e)),
                   wait=wait, inc=[("GPC", 1)])
        else:
            p.emit("gpsimd", fn, wait=wait, inc=[("GPC", 1)])
        return p.tok("GPC")

    def mm(fn, wait=()):
        p.emit("tensor", fn, wait=wait, inc=[("MM", 1)])
        return p.tok("MM")

    for dst, srcd in ((cosT, COS), (sinT, SIN), (permT, PERM), (iden, IDEN),
                      (idenb, IDENB), (ones64, ONES64), (ones1, ONES1),
                      (bqkv_t, BQKV_T), (bqkv_sh, BQKV_SH), (qthr, QTHR)):
        dma("sync", "LDC", dst.ap(), srcd)
    brow_stage = f32(o_wstg, 3 * C)[0:1, :]
    t_ld = dma("sync", "LDC", brow_stage, BROW)
    ve(lambda e: e.tensor_copy(brow_h.ap(), brow_stage), wait=[("LDC", t_ld)], drain=True)
    ve(lambda e: e.tensor_tensor(brow_l.ap(), brow_stage, brow_h.ap(), OP.subtract))

    def split_w_resident(src_dram, col0, ncols, o_h, o_l, tag):
        mm_guard = p.tok("MM")
        for cc in range(8):
            buf = cc % 2
            stg = f32(o_wstg + buf * C * 4, ncols)
            w = [("VE", split_w_resident.toks.get(buf, 0))]
            t_ld = dma("sync", f"LDW{buf}", stg,
                       src_dram[cc * 128:(cc + 1) * 128, col0:col0 + ncols], wait=w)
            hh = bf16v(o_h + cc * ncols * 2, ncols)
            ll = bf16v(o_l + cc * ncols * 2, ncols)
            ve(lambda e, hh=hh, stg=stg: e.tensor_copy(hh, stg),
               wait=[(f"LDW{buf}", t_ld), ("MM", mm_guard)], drain=True)
            tve = ve(lambda e, ll=ll, stg=stg, hh=hh:
                     e.tensor_tensor(ll, stg, hh, OP.subtract))
            split_w_resident.toks[buf] = tve
        return tve
    split_w_resident.toks = {}

    def phase1(b):
        p.barrier_all()
        if not hasattr(phase1, "stg_free"):
            phase1.stg_free = {0: 0, 1: 0}
            phase1.vst_free = {0: 0, 1: 0}
        w_first = split_w_resident(WQKV, 0, C, o_wwh, o_wwl, f"wq{b}")
        xload_free = {0: 0, 1: 0}
        bank_free = {0: 0, 1: 0, 2: 0, 3: 0}
        for tb in range(NTB):
            buf = tb % 2
            xl_ap = f32(o_xload + buf * C * 4, C)
            t_ld = dma("sync", f"LDX{buf}", xl_ap, X[b, tb * 128:(tb + 1) * 128, :],
                       wait=[("MM", xload_free[buf])])
            for half in range(2):
                bslot = (tb * 2 + half) % 4
                pbank = PSU[bslot // 2].ap()[:, (bslot % 2) * 512:(bslot % 2) * 512 + 512]
                for i4 in range(4):
                    cc = half * 4 + i4
                    t_mm = mm(lambda e, ps=pbank[:, i4 * 128:(i4 + 1) * 128],
                              xb=xl_ap[:, cc * 128:(cc + 1) * 128]:
                              e.transpose(ps, xb, iden.ap()),
                              wait=([(f"LDX{buf}", t_ld), ("VE", bank_free[bslot])]
                                    if i4 == 0 else ()))
                for i4 in range(4):
                    cc = half * 4 + i4
                    d_h = bf16v(o_xth + (cc * T + tb * 128) * 2, 128)
                    t_ac = sca(lambda e, d=d_h, s_=pbank[:, i4 * 128:(i4 + 1) * 128]:
                               e.activation(d, s_, AF.Copy),
                               wait=[("MM", t_mm)] if i4 == 0 else (),
                               drain=(i4 == 3))
                for i4 in range(4):
                    cc = half * 4 + i4
                    d_l = bf16v(o_xtl + (cc * T + tb * 128) * 2, 128)
                    h_ = bf16v(o_xth + (cc * T + tb * 128) * 2, 128)
                    tve = ve(lambda e, d=d_l, s_=pbank[:, i4 * 128:(i4 + 1) * 128], h=h_:
                             e.tensor_tensor(d, s_, h, OP.subtract),
                             wait=[("ACT", t_ac)] if i4 == 0 else ())
                bank_free[bslot] = tve
            xload_free[buf] = t_mm
        xbuild_ve = p.tok("VE")
        xbuild_act = p.tok("ACT")
        qk_slot_free = {i: 0 for i in range(6)}
        sh_free = 0
        stg_free = phase1.stg_free
        for wi, wave in enumerate(("q", "k")):
            if wi == 0:
                t_wsplit = w_first
            else:
                t_wsplit = split_w_resident(WQKV, wi * C, C, o_wwh, o_wwl, f"w{wave}{b}")
            for tt in range(NTT):
                for oc in range(8):
                    idx = tt * 8 + oc
                    slot = idx % 6
                    pbank = PSU[slot // 2].ap()[:, (slot % 2) * 512:(slot % 2) * 512 + Tt]
                    ops = []
                    for cc in range(8):
                        wh = bf16v(o_wwh + (cc * C + oc * 128) * 2, 128)
                        wl = bf16v(o_wwl + (cc * C + oc * 128) * 2, 128)
                        xh = bf16v(o_xth + (cc * T + tt * Tt) * 2, Tt)
                        xl = bf16v(o_xtl + (cc * T + tt * Tt) * 2, Tt)
                        ops += [(wh, xh), (wh, xl), (wl, xh)]
                    for i, (wa, xa) in enumerate(ops):
                        w0 = []
                        if i == 0:
                            w0 = [("VE", max(t_wsplit, qk_slot_free[slot], xbuild_ve)),
                                  ("ACT", xbuild_act)]
                        t_mm = mm(lambda e, pb=pbank, wa=wa, xa=xa,
                                  st=(i == 0), sp=(i == len(ops) - 1):
                                  e.matmul(pb, wa, xa, start=st, stop=sp), wait=w0)
                    occ = wi * 8 + oc
                    par = idx % 2
                    tq = f32(o_rtq + par * Tt * 4, Tt)
                    t1 = f32(o_rt1 + par * Tt * 4, Tt)
                    t2 = f32(o_rt2 + par * Tt * 4, Tt)
                    t3 = f32(o_rt3 + par * Tt * 4, Tt)
                    hh = bf16v(o_sth + par * Tt * 2, Tt)
                    ll = bf16v(o_stl + par * Tt * 2, Tt)
                    cosv = cosT.ap()[:, tt * Tt:(tt + 1) * Tt]
                    sinv = sinT.ap()[:, tt * Tt:(tt + 1) * Tt]
                    bias_ap = bqkv_t.ap()[:, occ:occ + 1]
                    tve1 = ve(lambda e, tq=tq, pb=pbank, ba=bias_ap:
                              e.tensor_scalar(tq, pb, ba, None, OP.add),
                              wait=[("MM", t_mm)], drain=True)
                    qk_slot_free[slot] = tve1
                    ve(lambda e, t1=t1, tq=tq, cv=cosv:
                       e.tensor_tensor(t1, tq, cv, OP.mult))
                    shslot = PSU[3].ap()[:, 512:512 + Tt]
                    t_mm2 = mm(lambda e, sh=shslot, tq=tq:
                               e.matmul(sh, permT.ap(), tq, start=True, stop=True),
                               wait=[("VE", max(tve1, sh_free))])
                    tve2 = ve(lambda e, t2=t2, sh=shslot, sv=sinv:
                              e.tensor_tensor(t2, sh, sv, OP.mult),
                              wait=[("MM", t_mm2)], drain=True)
                    sh_free = tve2
                    ve(lambda e, t3=t3, t1=t1, t2=t2: e.tensor_tensor(t3, t1, t2, OP.add),
                       drain=True)
                    t_ac = sca(lambda e, hh=hh, t3=t3: e.activation(hh, t3, AF.Copy),
                               wait=[(f"STR{par}", stg_free[par]), ("VE", p.tok("VE"))],
                               drain=True)
                    tve = ve(lambda e, ll=ll, t3=t3, hh=hh:
                             e.tensor_tensor(ll, t3, hh, OP.subtract),
                             wait=[("ACT", t_ac)])
                    DST_H = QS_H if wave == "q" else KS_H
                    DST_L = QS_L if wave == "q" else KS_L
                    dma("gpsimd", f"STR{par}", DST_H[b, oc, :, tt * Tt:(tt + 1) * Tt], hh,
                        wait=[("VE", tve)])
                    t_st = dma("gpsimd", f"STR{par}", DST_L[b, oc, :, tt * Tt:(tt + 1) * Tt], ll)
                    stg_free[par] = t_st
        t_wsplit = split_w_resident(WQKV, 2 * C, C, o_wwh, o_wwl, f"wv{b}")
        v_slot_free = {i: 0 for i in range(3)}
        vst_free = phase1.vst_free
        for sbk in range(NSC):
            slot = sbk % 3
            pb2 = PSU[slot].ap()
            for et in range(2):
                pbank = pb2[:, et * 512:(et + 1) * 512]
                ops = []
                for cc in range(8):
                    xh = bf16v(o_xth + (cc * T + sbk * 128) * 2, 128)
                    xl = bf16v(o_xtl + (cc * T + sbk * 128) * 2, 128)
                    wh = bf16v(o_wwh + (cc * C + et * 512) * 2, 512)
                    wl = bf16v(o_wwl + (cc * C + et * 512) * 2, 512)
                    ops += [(xh, wh), (xh, wl), (xl, wh)]
                for i, (sa, ma) in enumerate(ops):
                    w0 = ([("VE", max(t_wsplit, v_slot_free[slot])), ("ACT", xbuild_act)]
                          if i == 0 else [])
                    mm(lambda e, pb=pbank, sa=sa, ma=ma, st=(i == 0):
                       e.matmul(pb, sa, ma, start=st, stop=False), wait=w0)
                mm(lambda e, pb=pbank, et=et:
                   e.matmul(pb, ones1.ap(), brow_h.ap()[:, et * 512:(et + 1) * 512],
                            start=False, stop=False))
                t_mm = mm(lambda e, pb=pbank, et=et:
                          e.matmul(pb, ones1.ap(), brow_l.ap()[:, et * 512:(et + 1) * 512],
                                   start=False, stop=True))
            buf = sbk % 2
            vh = bf16v(o_vth + buf * C * 2, C)
            vl = bf16v(o_vtl + buf * C * 2, C)
            t_ac = sca(lambda e, vh=vh, pb=pb2: e.activation(vh, pb, AF.Copy),
                       wait=[("MM", t_mm), (f"STV{buf}", vst_free[buf])], drain=True)
            tve = ve(lambda e, vl=vl, pb=pb2, vh=vh:
                     e.tensor_tensor(vl, pb, vh, OP.subtract),
                     wait=[("ACT", t_ac)])
            v_slot_free[slot] = tve
            dma("gpsimd", f"STV{buf}", VS_H[b, sbk], vh, wait=[("VE", tve)])
            t_st = dma("gpsimd", f"STV{buf}", VS_L[b, sbk], vl)
            vst_free[buf] = t_st

    def phase2(b, g):
        p.barrier_all()
        t_k = {}
        for pp in range(PPG):
            pair = g * PPG + pp
            dma("sync", f"LDK{pp}", bf16v(o_kgh + pp * T * 2, T), KS_H[b, pair])
            t_k[pp] = dma("sync", f"LDK{pp}", bf16v(o_kgl + pp * T * 2, T),
                          KS_L[b, pair])
        for sbk in range(NSC):
            dma("sync", "LDS", bf16v(o_vgh + sbk * 512 * 2, 512),
                VS_H[b, sbk, :, g * 512:(g + 1) * 512])
            t_slab = dma("sync", "LDS", bf16v(o_vgl + sbk * 512 * 2, 512),
                         VS_L[b, sbk, :, g * 512:(g + 1) * 512])
        q_free = 0
        pv_free = {0: 0, 1: 0}
        an_free = {0: 0, 1: 0}
        exp_tok = {0: 0, 1: 0, 2: 0}
        spl_tok = {0: 0, 1: 0, 2: 0}
        spl_gp_tok = {0: 0, 1: 0, 2: 0}
        pvz_tok = {0: 0, 1: 0, 2: 0}
        for tt in range(NTT):
            for pp in range(PPG):
                dma("sync", "LDQ", bf16v(o_qth + pp * Tt * 2, Tt),
                    QS_H[b, g * PPG + pp, :, tt * Tt:(tt + 1) * Tt],
                    wait=[("MM", q_free)] if pp == 0 else ())
                t_q = dma("sync", "LDQ", bf16v(o_qtl + pp * Tt * 2, Tt),
                          QS_L[b, g * PPG + pp, :, tt * Tt:(tt + 1) * Tt])
            for pp in range(PPG):
                pvpar = 0
                pv_bank = PSU[3].ap()[:, 0:Tt]
                z_bank = PSU[3].ap()[:, 512:512 + Tt]
                qh = bf16v(o_qth + pp * Tt * 2, Tt)
                ql = bf16v(o_qtl + pp * Tt * 2, Tt)
                def s_chunk_mms(s):
                    par = s % 3
                    sps = PSU[par].ap()
                    kh = bf16v(o_kgh + (pp * T + s * 128) * 2, 128)
                    kl = bf16v(o_kgl + (pp * T + s * 128) * 2, 128)
                    ops = ((kh, qh), (kh, ql), (kl, qh))
                    for i, (ka, qa) in enumerate(ops):
                        for hi2 in range(2):
                            r0 = hi2 * 64
                            sbank = sps[:, hi2 * 512: hi2 * 512 + Tt]
                            tp = (hi2 * 64, 0)
                            w0 = []
                            if i == 0 and hi2 == 0:
                                w0 = [(f"LDK{pp}", t_k[pp]), ("LDQ", t_q),
                                      ("ACT", exp_tok[par])]
                            t_mm = mm(lambda e, sb_=sbank, ka=ka[r0:r0 + 64, :],
                                      qa=qa[r0:r0 + 64, :], st=(i == 0), sp=(i == 2), tp=tp:
                                      e.matmul(sb_, ka, qa, start=st, stop=sp,
                                               tile_position=tp), wait=w0)
                    return t_mm
                def s_chunk_tail(s, first_pv, last_pv):
                    par = s % 3
                    sps = PSU[par].ap()
                    e32 = f32(o_e32 + par * 1024 * 4, 2 * Tt)
                    e32b = main.ap().bitcast(BF16)
                    c0 = (o_e32 + par * 1024 * 4) // 2
                    eh = e32b[:, c0 + 1: c0 + 4 * Tt: 2]
                    el = bf16v(o_el + par * 1024 * 2, 2 * Tt)
                    if Tt == 512:
                        t_act = sca(lambda e, o=e32, i=sps:
                                    e.activation(o, i, AF.Exp, bias=0.0, scale=0.125),
                                    wait=[("MM", smm_tok[s]), ("VE", spl_tok[par]),
                                          ("GPC", spl_gp_tok[par])])
                    else:
                        sca(lambda e, o=e32[:, 0:Tt], i=sps[:, 0:Tt]:
                            e.activation(o, i, AF.Exp, bias=0.0, scale=0.125),
                            wait=[("MM", smm_tok[s]), ("VE", spl_tok[par]),
                                  ("GPC", spl_gp_tok[par])])
                        t_act = sca(lambda e, o=e32[:, Tt:2 * Tt], i=sps[:, 512:512 + Tt]:
                                    e.activation(o, i, AF.Exp, bias=0.0, scale=0.125))
                    exp_tok[par] = t_act
                    t_spl = ve(lambda e, el=el, e32=e32, eh=eh:
                               e.tensor_tensor(el, e32, eh, OP.subtract),
                               wait=[("ACT", t_act), ("MM", pvz_tok[par])])
                    spl_tok[par] = t_spl
                    wspl = ("VE", t_spl)
                    pvz = []
                    for hi2 in range(2):
                        ehh = eh[:, hi2 * Tt:(hi2 + 1) * Tt]
                        ell = el[:, hi2 * Tt:(hi2 + 1) * Tt]
                        vh_ = bf16v(o_vgh + (s * 512 + pp * 128 + hi2 * 64) * 2, 64)
                        vl_ = bf16v(o_vgl + (s * 512 + pp * 128 + hi2 * 64) * 2, 64)
                        tp = (0, hi2 * 64)
                        pvout = pv_bank[hi2 * 64: hi2 * 64 + 64, :]
                        zout = z_bank[hi2 * 64: hi2 * 64 + 64, :]
                        pvz.append((pvout, zout, tp,
                                    ((vh_, ehh), (vl_, ehh), (vh_, ell)), (ehh, ell)))
                    for i3 in range(3):
                        for hi2 in range(2):
                            pvout, zout, tp, pvops, zops = pvz[hi2]
                            va, ea = pvops[i3]
                            w0 = []
                            if i3 == 0 and hi2 == 0:
                                w0 = [("ACT", t_act)]
                                if first_pv:
                                    w0.append(("VE", pv_free[pvpar]))
                                    if tt == 0 and pp == 0:
                                        w0.append(("LDS", t_slab))
                            if i3 == 2 and hi2 == 0:
                                w0 = [wspl]
                            mm(lambda e, o=pvout, va=va, ea=ea,
                               st=(first_pv and i3 == 0), sp=(last_pv and i3 == 2), tp=tp:
                               e.matmul(o, va, ea, start=st, stop=sp,
                                        tile_position=tp, skip_group_check=True), wait=w0)
                    for i2 in range(2):
                        for hi2 in range(2):
                            pvout, zout, tp, pvops, zops = pvz[hi2]
                            ea = zops[i2]
                            t_mm = mm(lambda e, o=zout, ea=ea,
                                      st=(first_pv and i2 == 0), sp=(last_pv and i2 == 1), tp=tp:
                                      e.matmul(o, ones64.ap(), ea, start=st,
                                               stop=sp, tile_position=tp,
                                               skip_group_check=True))
                    pvz_tok[par] = t_mm
                    return t_mm
                smm_tok = {}
                smm_tok[0] = s_chunk_mms(0)
                if NSC > 1:
                    smm_tok[1] = s_chunk_mms(1)
                for s in range(NSC):
                    if s + 2 < NSC:
                        smm_tok[s + 2] = s_chunk_mms(s + 2)
                    s_chunk_tail(s, first_pv=(s == 0), last_pv=(s == NSC - 1))
                t_mmz = p.tok("MM")
                zrv = f32(o_zr + pvpar * 512 * 4, Tt)
                anf = f32(o_anf + pvpar * 512 * 4, Tt)
                anh = bf16v(o_anh + pvpar * 512 * 2, Tt)
                anl = bf16v(o_anl + pvpar * 512 * 2, Tt)
                ve(lambda e, zr=zrv, zb=z_bank: e.reciprocal(zr, zb),
                   wait=[("MM", t_mmz), (f"STA{pvpar}", an_free[pvpar])], drain=True)
                tvm = ve(lambda e, o=anf, pv=pv_bank, zr=zrv:
                         e.tensor_tensor(o, pv, zr, OP.mult), drain=True)
                t_ac = sca(lambda e, anh=anh, anf=anf: e.activation(anh, anf, AF.Copy),
                           wait=[("VE", tvm)], drain=True)
                tve = ve(lambda e, anl=anl, anf=anf, anh=anh:
                         e.tensor_tensor(anl, anf, anh, OP.subtract),
                         wait=[("ACT", t_ac)])
                pv_free[pvpar] = tve
                pair = g * PPG + pp
                dma("gpsimd", f"STA{pvpar}", AT_H[b, pair, :, tt * Tt:(tt + 1) * Tt], anh,
                    wait=[("VE", tve)])
                t_st = dma("gpsimd", f"STA{pvpar}", AT_L[b, pair, :, tt * Tt:(tt + 1) * Tt], anl)
                an_free[pvpar] = t_st
            q_free = p.tok("MM")

    def quant_chain(src, dst, ncols, wait0=(), gp_ladder=False):
        nb = ncols // 32
        am = f32(o_am, nb); rsc = f32(o_rsc, nb); sc_ = f32(o_sc2, nb)
        sq = f32(o_sq, ncols); qb = f32(o_qb, ncols); qd = f32(o_qd, ncols)
        r3 = lambda a: a.rearrange("p (x y) -> p x y", x=nb)
        ve(lambda e: e.tensor_reduce(am, r3(src), AX.X, OP.max,
                                     apply_absolute_value=True), wait=wait0, drain=True)
        ve(lambda e: e.reciprocal(rsc, am), drain=True)
        ve(lambda e: e.tensor_scalar(rsc, rsc, 7.5, 1e30, OP.mult, OP.min), drain=True)
        ve(lambda e: e.reciprocal(sc_, rsc), drain=True)
        rb = rsc.unsqueeze(2).broadcast_to([128, nb, 32])
        scb = sc_.unsqueeze(2).broadcast_to([128, nb, 32])
        ve(lambda e: e.tensor_tensor(r3(qb), r3(src), rb, OP.mult), drain=True)
        ve(lambda e: e._custom_dve(OP_QINVS6, out=sq, in0=qb, s0=2.0, s1=4.0),
           drain=True)
        ve(lambda e: e.tensor_tensor(qd, qb, sq, OP.mult), drain=True)
        ve(lambda e: e.tensor_single_scalar(qb, qd, 16.0, OP.add), drain=True)
        for i, w in enumerate((32.0, 8.0, 2.0)):
            src_u = qb if i == 0 else qd
            ve(lambda e, w=w, s_=src_u, i=i: e._custom_dve(
                OP_LADDER2, out=qd, in0=s_, s0=w - 0.5, s1=w,
                imm2=w / 2.0 - 0.5, in1=qthr.ap()[:, 4 + i:5 + i]), drain=True)
        ve(lambda e: e.tensor_tensor(qd, qb, qd, OP.subtract), drain=True)
        ve(lambda e: e.tensor_scalar(qb, sq, 0.015625, -0.21875, OP.mult, OP.add),
           drain=True)
        ve(lambda e: e.tensor_tensor(sq, qb, sq, OP.mult), drain=True)
        ve(lambda e: e.tensor_single_scalar(sq, sq, 0.875, OP.add), drain=True)
        ve(lambda e: e.scalar_tensor_tensor(qd, qd, -16.0, sq, OP.add, OP.mult),
           drain=True)
        return ve(lambda e: e.tensor_tensor(r3(dst), r3(qd), scb, OP.mult),
                  drain=True)

    def layernorm2(srcs, dsts, jcols, dummies=None, wait_first=()):
        st = f32(o_st1, 16)
        nj = len(srcs)
        if dummies is None:
            dummies = [f32(o_sq, C)] * nj
        for j, (s_, jc) in enumerate(zip(srcs, jcols)):
            sca(lambda e, s_=s_, jc=jc, d_=dummies[j]:
                e.activation(d_, s_, AF.Square,
                             accum_out=st[:, 4 + jc:5 + jc]),
                wait=[("VE", p.tok("VE"))] + (list(wait_first) if j == 0 else []),
                drain=True)
        for j, (s_, jc) in enumerate(zip(srcs, jcols)):
            t_act = sca(lambda e, s_=s_, jc=jc, d_=dummies[j]:
                        e.activation(d_, s_, AF.Identity,
                                     accum_out=st[:, 0 + jc:1 + jc]), drain=True)
        p.emit("vector", lambda e: None, wait=[("ACT", t_act)])
        c_sum = st[:, 0:nj]; c_ssq = st[:, 4:4 + nj]
        c_mu = st[:, 8:8 + nj]; c_v = st[:, 12:12 + nj]
        ve(lambda e: e.tensor_single_scalar(c_mu, c_sum, 1.0 / C, OP.mult), drain=True)
        ve(lambda e: e.tensor_single_scalar(c_v, c_ssq, 1.0 / C, OP.mult), drain=True)
        ve(lambda e: e.tensor_tensor(c_sum, c_mu, c_mu, OP.mult), drain=True)
        ve(lambda e: e.tensor_tensor(c_v, c_v, c_sum, OP.subtract), drain=True)
        t0 = ve(lambda e: e.tensor_single_scalar(c_v, c_v, 1e-5, OP.add), drain=True)
        t_act = sca(lambda e: e.activation(c_ssq, c_v, AF.Sqrt), wait=[("VE", t0)],
                    drain=True)
        ve(lambda e: e.reciprocal(c_v, c_ssq), wait=[("ACT", t_act)], drain=True)
        tl = 0
        for j, (s_, d_, jc) in enumerate(zip(srcs, dsts, jcols)):
            tl = ve(lambda e, s_=s_, d_=d_, jc=jc:
                    e.tensor_scalar(d_, s_, st[:, 8 + jc:9 + jc], st[:, 12 + jc:13 + jc],
                                    OP.subtract, OP.mult), drain=(j == nj - 1))
        return tl

    def phase3():
        p.barrier_all()
        for cc in range(8):
            buf = cc % 2
            stg = f32(o_wstg3 + buf * C * 4, C)
            w = [("VE", phase3.wtok.get(buf, 0))]
            t_ld = dma("sync", f"LDW{buf}", stg, WOUT[cc * 128:(cc + 1) * 128, :], wait=w)
            hh = bf16v(o_woh + cc * C * 2, C)
            ll = bf16v(o_wol + cc * C * 2, C)
            ve(lambda e, hh=hh, stg=stg: e.tensor_copy(hh, stg),
               wait=[(f"LDW{buf}", t_ld)], drain=True)
            tve = ve(lambda e, ll=ll, stg=stg, hh=hh:
                     e.tensor_tensor(ll, stg, hh, OP.subtract))
            phase3.wtok[buf] = tve
        tsl_free = {0: 0, 1: 0}
        def psb16(sl):
            return PSU[3].ap().bitcast(BF16)[:, sl * 1024: sl * 1024 + 128]
        for obp in range(4):
            stg = f32(o_wstg3, 2 * C)
            stgv = stg.rearrange("p (a c) -> p a c", a=2)
            win = WFFN[2 * obp * 128:(2 * obp + 2) * 128, :].rearrange(
                "(a q) c -> q a c", q=128)
            t_ld = dma("sync", "LDW0", stgv, win,
                       wait=[("VE", max(phase3.wtok.get(0, 0), phase3.wtok.get(1, 0)))])
            xdq = f32(o_qb, 2 * C)
            tq = quant_chain(stg, xdq, 2 * C,
                             wait0=[("LDW0", t_ld),
                                    ("ACT", max(tsl_free.values()))])
            xdh = bf16v(o_xdh, 2 * C); xdl = bf16v(o_xdl, 2 * C)
            t_h = sca(lambda e, d=xdh, s_=xdq: e.activation(d, s_, AF.Copy),
                      wait=[("VE", tq)], drain=True)
            tve = ve(lambda e, d=xdl, s_=xdq, h=xdh:
                     e.tensor_tensor(d, s_, h, OP.subtract),
                     wait=[("ACT", t_h)], drain=True)
            phase3.wtok[0] = tve
            for j in range(2):
                ob = 2 * obp + j
                for half, srcv, o_dst in (("h", xdh, o_wfh), ("l", xdl, o_wfl)):
                    for cc in range(8):
                        sl = cc % 2
                        psl = psb16(sl)
                        t_mm = mm(lambda e, ps=psl,
                                  xb=srcv[:, j * C + cc * 128: j * C + (cc + 1) * 128]:
                                  e.transpose(ps, xb, idenb.ap()),
                                  wait=[("VE", tve), ("ACT", tsl_free[sl])])
                        d = bf16v(o_dst + (cc * C + ob * 128) * 2, 128)
                        last = (j == 1 and half == "l" and cc == 7)
                        t2 = sca(lambda e, d=d, s_=psl: e.activation(d, s_, AF.Copy),
                                 wait=[("MM", t_mm)], drain=last)
                        tsl_free[sl] = t2
        prep_guard = [("LDW0", p.tok("LDW0")), ("LDW1", p.tok("LDW1")),
                      ("VE", p.tok("VE"))]
        first_at = True
        x3_free = {0: 0, 1: 0}
        at_free = 0
        y_free = {0: 0, 1: 0}
        f_free = 0
        out_free = 0

        def stage1(b, tt, half, t_at, gidx):
            apar = gidx % 2
            atbase_h = o_ath + apar * (NPAIR * 256 * 2)
            atbase_l = o_atl + apar * (NPAIR * 256 * 2)
            if not hasattr(stage1, "first"):
                stage1.first = True
            """Wout 3-pass + bias + residual for 2 token blocks -> sres[par]."""
            nonlocal x3_free
            par = gidx % 2
            tb0 = tt * (Tt // 128) + half * 2
            t0_tok = tb0 * 128
            x3 = f32(o_x3, 2 * C)
            x3v = x3.rearrange("p (a c) -> p a c", a=2)
            xin = X[b, t0_tok:t0_tok + 256, :].rearrange("(a q) c -> q a c", q=128)
            wx = [("VE", max(x3_free.values()))]
            if stage1.first:
                wx += prep_guard
                stage1.first = False
            t_x = dma("sync", f"LDX{par}", x3v, xin, wait=wx)
            sres = f32(sres_off(gidx % 3), 2 * C)
            for j in range(2):
                tb4 = half * 2 + j
                ypar = j % 2
                ypsum = PSU[ypar].ap()
                for ot in range(2):
                    yb = ypsum[:, ot * 512:(ot + 1) * 512]
                    ops = []
                    for cc in range(8):
                        ah = bf16v(atbase_h + (cc * 256 + j * 128) * 2, 128)
                        al = bf16v(atbase_l + (cc * 256 + j * 128) * 2, 128)
                        wh = bf16v(o_woh + (cc * C + ot * 512) * 2, 512)
                        wl = bf16v(o_wol + (cc * C + ot * 512) * 2, 512)
                        ops += [(ah, wh), (ah, wl), (al, wh)]
                    for i, (sa, ma) in enumerate(ops):
                        w0 = []
                        if i == 0:
                            w0 = [(f"LDA{apar}", t_at[0]), (f"LAG{apar}", t_at[1]),
                                  ("VE", y_free[ypar])]
                        mm(lambda e, pb=yb, sa=sa, ma=ma, st=(i == 0):
                           e.matmul(pb, sa, ma, start=st, stop=False), wait=w0)
                    mm(lambda e, pb=yb, ot=ot:
                       e.matmul(pb, ones1.ap(),
                                brow_h.ap()[:, 2 * C + ot * 512: 2 * C + (ot + 1) * 512],
                                start=False, stop=False))
                    t_mm = mm(lambda e, pb=yb, ot=ot:
                              e.matmul(pb, ones1.ap(),
                                       brow_l.ap()[:, 2 * C + ot * 512: 2 * C + (ot + 1) * 512],
                                       start=False, stop=True))
                tve = ve(lambda e, s_=sres[:, j * C:(j + 1) * C], pb=ypsum,
                         x3j=x3[:, j * C:(j + 1) * C]:
                         e.tensor_tensor(s_, pb, x3j, OP.add),
                         wait=[("MM", t_mm), (f"LDX{par}", t_x)],
                         drain=(j == 1))
                y_free[ypar] = tve
            x3_free[par] = tve
            return sres

        def stage2a(b, tt, half, sres, gidx):
            x1t = f32(o_x1t if gidx % 2 == 0 else o_x1t2, 2 * C)
            layernorm2([sres[:, 0:C], sres[:, C:2 * C]],
                       [x1t[:, 0:C], x1t[:, C:2 * C]], [0, 1])
            xdq = f32(o_qb, 2 * C)
            tq = quant_chain(x1t, xdq, 2 * C, gp_ladder=GP_LADDER)
            xdh = bf16v(o_xdh, 2 * C); xdl = bf16v(o_xdl, 2 * C)
            t_h = sca(lambda e, d=xdh, s_=xdq: e.activation(d, s_, AF.Copy),
                      wait=[("VE", tq)], drain=True)
            tve = ve(lambda e, d=xdl, s_=xdq, h=xdh:
                     e.tensor_tensor(d, s_, h, OP.subtract),
                     wait=[("ACT", t_h)], drain=True)
            return x1t, tve, t_h

        def stage2b(b, tt, half, sres, gidx, x1t, tve, t_h):
            tpar = gidx % 2
            o_xdth = o_xdtd + tpar * 2 * (2 * C * 2)
            o_xdtl = o_xdth + 2 * C * 2
            xdh = bf16v(o_xdh, 2 * C); xdl = bf16v(o_xdl, 2 * C)
            tsl = {0: 0, 1: 0}
            def psb16_t(sl):
                return PSU[3].ap().bitcast(BF16)[:, sl * 1024: sl * 1024 + 128]
            for j in range(2):
                for hN, srcv, o_dst in (("h", xdh, o_xdth), ("l", xdl, o_xdtl)):
                    for cc in range(8):
                        sl = cc % 2
                        psl = psb16_t(sl)
                        wt = ([("ACT", max(t_h, tsl[sl]))] if hN == "h" else
                              [("VE", tve), ("ACT", tsl[sl])])
                        t_mm = mm(lambda e, ps=psl,
                                  xb=srcv[:, j * C + cc * 128: j * C + (cc + 1) * 128]:
                                  e.transpose(ps, xb, idenb.ap()),
                                  wait=wt)
                        d = bf16v(o_dst + (j * C + cc * 128) * 2, 128)
                        last = (j == 1 and hN == "l" and cc == 7)
                        t2 = sca(lambda e, d=d, s_=psl: e.activation(d, s_, AF.Copy),
                                 wait=[("MM", t_mm)], drain=last)
                        tsl[sl] = t2
            return p.tok("ACT")

        def stage3(b, tt, half, sres, gidx, x1t, t_xdt):
            nonlocal f_free, out_free
            tpar = gidx % 2
            o_xdth = o_xdtd + tpar * 2 * (2 * C * 2)
            o_xdtl = o_xdth + 2 * C * 2
            tb0 = tt * (Tt // 128) + half * 2
            t0_tok = tb0 * 128
            o3 = f32(o_o3, 2 * C)
            for j in range(2):
                fpsum = PSU[2].ap()
                for ot in range(2):
                    fb = fpsum[:, ot * 512:(ot + 1) * 512]
                    ops = []
                    for cc in range(8):
                        xh = bf16v(o_xdth + (j * C + cc * 128) * 2, 128)
                        xl = bf16v(o_xdtl + (j * C + cc * 128) * 2, 128)
                        wh = bf16v(o_wfh + (cc * C + ot * 512) * 2, 512)
                        wl = bf16v(o_wfl + (cc * C + ot * 512) * 2, 512)
                        ops += [(xh, wh), (xh, wl), (xl, wh)]
                    for i, (sa, ma) in enumerate(ops):
                        w0 = []
                        if i == 0:
                            w0 = [("ACT", t_xdt), ("VE", f_free)]
                        mm(lambda e, pb=fb, sa=sa, ma=ma, st=(i == 0):
                           e.matmul(pb, sa, ma, start=st, stop=False), wait=w0)
                    mm(lambda e, pb=fb, ot=ot:
                       e.matmul(pb, ones1.ap(),
                                brow_h.ap()[:, C + ot * 512: C + (ot + 1) * 512],
                                start=False, stop=False))
                    t_mmf = mm(lambda e, pb=fb, ot=ot:
                               e.matmul(pb, ones1.ap(),
                                        brow_l.ap()[:, C + ot * 512: C + (ot + 1) * 512],
                                        start=False, stop=True))
                tve = ve(lambda e, s_=sres[:, j * C:(j + 1) * C], pb=fpsum,
                         x1j=x1t[:, j * C:(j + 1) * C]:
                         e.tensor_tensor(s_, pb, x1j, OP.add),
                         wait=[("MM", t_mmf)],
                         drain=True)
                f_free = tve
            layernorm2([sres[:, 0:C], sres[:, C:2 * C]],
                       [o3[:, 0:C], o3[:, C:2 * C]], [0, 1],
                       dummies=[o3[:, 0:C], o3[:, C:2 * C]],
                       wait_first=[("STO0", out_free)])
            tvo = p.tok("VE")
            o3v = o3.rearrange("p (a c) -> p a c", a=2)
            oout = OUTX[b, t0_tok:t0_tok + 256, :].rearrange("(a q) c -> q a c", q=128)
            out_free = dma("gpsimd", "STO0", oout, o3v, wait=[("VE", tvo)])

        pend_s2 = None
        pend_s3 = None
        groups = [(b, tt, half) for b in range(Bb) for tt in range(NTT)
                  for half in range(Tt // 256)]
        at_mm_tok = {0: 0, 1: 0}
        at_ld_tok = {}

        def load_at(gidx, first=False):
            gb, gtt, ghalf = groups[gidx]
            apar = gidx % 2
            t0_tok = gtt * Tt + ghalf * 256
            w0 = [("MM", at_mm_tok[apar])] + (prep_guard if first else [])
            dsth = bf16v(o_ath + apar * (NPAIR * 256 * 2),
                         NPAIR * 256).rearrange("q (a c) -> q a c", a=NPAIR)
            dstl = bf16v(o_atl + apar * (NPAIR * 256 * 2),
                         NPAIR * 256).rearrange("q (a c) -> q a c", a=NPAIR)
            th = dma("sync", f"LDA{apar}", dsth,
                     AT_H[gb, :, :, t0_tok:t0_tok + 256].rearrange("a q c -> q a c"),
                     wait=w0)
            tl = dma("gpsimd", f"LAG{apar}", dstl,
                     AT_L[gb, :, :, t0_tok:t0_tok + 256].rearrange("a q c -> q a c"),
                     wait=list(w0))
            at_ld_tok[gidx] = (th, tl)

        load_at(0, first=True)
        for gidx, (b, tt, half) in enumerate(groups):
            if gidx + 1 < len(groups):
                load_at(gidx + 1)
            a_out = None
            if pend_s2 is not None:
                a_out = stage2a(*pend_s2)
            sres = stage1(b, tt, half, at_ld_tok[gidx], gidx)
            at_mm_tok[gidx % 2] = p.tok("MM")
            if pend_s3 is not None:
                stage3(*pend_s3)
            if pend_s2 is not None:
                t_xdt = stage2b(*pend_s2, *a_out)
                pend_s3 = pend_s2 + (a_out[0], t_xdt)
            pend_s2 = (b, tt, half, sres, gidx)
        if pend_s2 is not None:
            a_out = stage2a(*pend_s2)
            if pend_s3 is not None:
                stage3(*pend_s3)
            t_xdt = stage2b(*pend_s2, *a_out)
            stage3(*(pend_s2 + (a_out[0], t_xdt)))
        p.emit("gpsimd", lambda e: e.nop(),
               wait=[("STO0", p.tok("STO0"))])
    phase3.wtok = {}

    for b in range(Bb):
        phase1(b)
        for g in range(NG):
            phase2(b, g)
    phase3()

    with nc.Block() as block:
        @block.sync
        def _(eng):
            p.replay("sync", eng)
        @block.tensor
        def _(eng):
            p.replay("tensor", eng)
        @block.vector
        def _(eng):
            p.replay("vector", eng)
        @block.scalar
        def _(eng):
            p.replay("scalar", eng)
        @block.gpsimd
        def _(eng):
            p.replay("gpsimd", eng)
    mybir.codegen_inst_isa_subclasses(nc)
    nc._ctx.close()
    return nc




_T_FULL = 2048

def kernel(L0_attn_x, L0_attn_Wqkv, L0_attn_bqkv, L0_attn_Wout, L0_attn_bout,
           L0_ffn_weight, L0_ffn_bias):
    from concourse.bass_utils import run_bass_kernel_spmd
    x = np.ascontiguousarray(np.asarray(L0_attn_x, dtype=np.float32))
    Wqkv = np.ascontiguousarray(np.asarray(L0_attn_Wqkv, dtype=np.float32))
    bqkv = np.ascontiguousarray(np.asarray(L0_attn_bqkv, dtype=np.float32))
    Wout = np.ascontiguousarray(np.asarray(L0_attn_Wout, dtype=np.float32))
    bout = np.ascontiguousarray(np.asarray(L0_attn_bout, dtype=np.float32))
    Wffn = np.ascontiguousarray(np.asarray(L0_ffn_weight, dtype=np.float32))
    bffn = np.ascontiguousarray(np.asarray(L0_ffn_bias, dtype=np.float32))
    B_full, T, Cc = x.shape
    assert (B_full, T, Cc) == (16, _T_FULL, 1024), (x.shape,)
    n_cores = 8
    nc = build_core_program(T=T, Bb=B_full // n_cores)
    in_maps = []
    for c in range(n_cores):
        xs = x[c * 2:(c + 1) * 2]
        in_maps.append(host_inputs_for_core(xs, Wqkv, bqkv, Wout, bout,
                                            Wffn, bffn, T))
    res = run_bass_kernel_spmd(nc, in_maps, core_ids=list(range(n_cores)))
    out = np.concatenate([np.asarray(r["out"], dtype=np.float32)
                          for r in res.results], axis=0)
    return out


# version 20
# speedup vs baseline: 1.0290x; 1.0290x over previous
import contextlib
import numpy as np
import ml_dtypes
import concourse.bass as bass
import concourse.mybir as mybir
import concourse.dve_ops as dve_ops
from concourse.dve_op import DveOp
from concourse.dve_spec import (
    Spec, Src0, C0, C1, C2, C3, Zero, select, maxx, lower as dve_lower,
)
from concourse.dve_table_gen import dve_ver_for

F32 = mybir.dt.float32
BF16 = mybir.dt.bfloat16
OP = mybir.AluOpType
AF = mybir.ActivationFunctionType
AX = mybir.AxisListType
BF = ml_dtypes.bfloat16

C, H, Dh = 1024, 16, 64
NPAIR, NG = 8, 2
PPG = NPAIR // NG


def _register_custom_dve():
    if any(o.name == "QINVS6" for o in dve_ops.OPS):
        return
    ver = dve_ver_for("TRN2")
    a = maxx(Src0, Zero - Src0)
    c1 = a >= C0
    c2 = a >= C1
    spec_q = Spec(body=(C1 + C1) - (c1 * C1) - (c2 * C0))

    def ref_q(in0, in1, s0, s1, imm2):
        aa = np.abs(in0)
        cc1 = (aa >= s0).astype(in0.dtype)
        cc2 = (aa >= s1).astype(in0.dtype)
        return (s1 + s1) - cc1 * s1 - cc2 * s0

    u1 = select(Src0 >= C0, Src0 - C1, Src0)
    spec_l = Spec(body=select(u1 >= C2, u1 - C3, u1))

    def ref_l(in0, in1, s0, s1, imm2):
        w2 = np.asarray(in1)
        uu1 = np.where(in0 >= s0, in0 - s1, in0)
        return np.where(uu1 >= imm2, uu1 - w2, uu1)

    for name, spec, ref in (("QINVS6", spec_q, ref_q), ("LADDER2", spec_l, ref_l)):
        lowered = dve_lower(spec, ver=ver, name=name)
        op = DveOp(name=name, spec=spec, subdim=False,
                   uops_sha={ver: lowered.sha(ver)}, reference=ref)
        dve_ops.OPS.append(op)
        dve_ops._SUB_OPCODE_FOR_NAME[name] = (
            dve_ops._CUSTOM_DVE_ROW_BASE + len(dve_ops.OPS) - 1)
        dve_ops.CUSTOM_DVE_SPECS[name] = spec
        dve_ops.CUSTOM_DVE_REFS[name] = ref


_register_custom_dve()
OP_QINVS6 = next(o for o in dve_ops.OPS if o.name == "QINVS6")
OP_LADDER2 = next(o for o in dve_ops.OPS if o.name == "LADDER2")


class Plan:
    ENGINES = ("sync", "tensor", "vector", "scalar", "gpsimd")
    def __init__(self, nc):
        self.nc = nc
        self.steps = []
        self.sems = {}
        self.cnt = {}
        self.kbar = 0
        self.dma_sems = {}
        for s in ("MM", "ACT", "VE", "GPC", "PH"):
            self.sem(s)
    def sem(self, name):
        if name not in self.sems:
            self.sems[name] = self.nc._ctx.enter_context(self.nc.semaphore(name))
            self.cnt[name] = 0
        return self.sems[name]
    def tok(self, name):
        self.sem(name)
        return self.cnt[name]
    def emit(self, engine, fn, wait=(), inc=()):
        for s, _ in wait:
            self.sem(s)
        wait = [(s, v) for (s, v) in wait if v > 0]
        for s, n in inc:
            self.sem(s)
            self.cnt[s] += n
            if n == 16:
                self.dma_sems.setdefault(engine, set()).add(s)
        self.steps.append((engine, fn, tuple(wait), tuple(inc)))
    def barrier_all(self):
        self.kbar += 1
        for en in self.ENGINES:
            own = [(s, self.cnt[s]) for s in sorted(self.dma_sems.get(en, ()))
                   if self.cnt[s] > 0]
            self.cnt["PH"] += 1
            self.steps.append((en, ("PHINC",), tuple(own), ()))
        tgt = 5 * self.kbar
        for en in self.ENGINES:
            self.steps.append((en, lambda e: None, (("PH", tgt),), ()))
    def replay(self, engine, eng):
        for e, fn, waits, incs in self.steps:
            if e != engine:
                continue
            if isinstance(fn, tuple) and fn[0] == "PHINC":
                for s, v in waits:
                    eng.wait_ge(self.sems[s], v)
                eng.drain()
                eng.sem_inc(self.sems["PH"], 1)
                continue
            for s, v in waits:
                eng.wait_ge(self.sems[s], v)
            ins = fn(eng)
            for s, n in incs:
                assert ins is not None
                ins = ins.then_inc(self.sems[s], n)


def host_constants(T, D=64):
    inv_freq = 1.0 / (10000.0 ** (np.arange(0, D, 2, dtype=np.float64) / D))
    pos = np.arange(T, dtype=np.float64)
    freqs = pos[:, None] * inv_freq[None, :]
    emb = np.concatenate([freqs, freqs], axis=-1)
    sin = np.sin(emb).astype(np.float32)
    cos = np.cos(emb).astype(np.float32)
    cosT = np.tile(cos.T, (2, 1)).astype(np.float32)
    sgn = np.where(np.arange(64) < 32, -1.0, 1.0).astype(np.float32)
    sinT = np.tile((sin.T * sgn[:, None]), (2, 1)).astype(np.float32)
    permT = np.zeros((128, 128), dtype=np.float32)
    for blk in range(2):
        for j in range(64):
            s = 2 * j + 1 if j < 32 else 2 * (j - 32)
            permT[blk * 64 + s, blk * 64 + j] = 1.0
    iden = np.eye(128, dtype=np.float32)
    idenb = np.eye(128, dtype=BF)
    ones64 = np.ones((128, 64), dtype=BF)
    ones1 = np.ones((1, 128), dtype=BF)
    return dict(cosT=cosT, sinT=sinT, permT=permT, iden=iden, idenb=idenb,
                ones64=ones64, ones1=ones1)


def host_inputs_for_core(x_core, Wqkv, bqkv, Wout, bout, Wffn, bffn, T):
    cst = host_constants(T)
    bqkv_t = np.zeros((128, 24), dtype=np.float32)
    for oc in range(24):
        bqkv_t[:, oc] = bqkv[oc * 128:(oc + 1) * 128]
    src = np.array([2 * j + 1 if j < 32 else 2 * (j - 32) for j in range(64)])
    srcs = np.concatenate([src, src + 64])
    bqkv_sh = np.zeros((128, 16), dtype=np.float32)
    for oc in range(16):
        col = bqkv_t[:, oc]
        bqkv_sh[:, oc] = col[srcs]
    brow = np.zeros((1, 3 * C), dtype=np.float32)
    brow[0, 0:C] = bqkv[2 * C:3 * C]
    brow[0, C:2 * C] = bffn
    brow[0, 2 * C:3 * C] = bout
    return {
        "x": np.ascontiguousarray(x_core),
        "wqkv": np.ascontiguousarray(Wqkv), "wout": np.ascontiguousarray(Wout),
        "wffn": np.ascontiguousarray(Wffn),
        "i_bqkv_t": bqkv_t, "i_bqkv_sh": bqkv_sh, "brow": brow,
        "i_cosT": cst["cosT"], "i_sinT": cst["sinT"], "i_permT": cst["permT"],
        "i_iden": cst["iden"], "i_idenb": cst["idenb"],
        "i_ones64": cst["ones64"], "i_ones1": cst["ones1"],
        "i_qthr": np.tile(np.array([[-2.0, -4.0, -31.5, 0.0, 16.0, 4.0, 1.0]],
                                   np.float32), (128, 1)),
    }


GP_LADDER = False


def build_core_program(T=2048, Bb=2):
    Tt = min(T, 512)
    NTT = T // Tt
    NSC = T // 128
    NTB = T // 128

    nc = bass.Bass("TRN2", target_bir_lowering=False)
    nc._ctx = contextlib.ExitStack()

    def din(name, shape, dt=F32):
        return nc.dram_tensor(name, shape, dt, kind="ExternalInput").ap()
    def dout(name, shape, dt=F32):
        return nc.dram_tensor(name, shape, dt, kind="ExternalOutput").ap()
    def dtmp(name, shape, dt):
        return nc.dram_tensor(name, shape, dt, kind="Internal").ap()

    X = din("x", [Bb, T, C])
    WQKV = din("wqkv", [C, 3 * C])
    WOUT = din("wout", [C, C])
    WFFN = din("wffn", [C, C])
    BQKV_T = din("i_bqkv_t", [128, 24])
    BQKV_SH = din("i_bqkv_sh", [128, 16])
    BROW = din("brow", [1, 3 * C])
    COS = din("i_cosT", [128, T])
    SIN = din("i_sinT", [128, T])
    PERM = din("i_permT", [128, 128])
    IDEN = din("i_iden", [128, 128])
    IDENB = din("i_idenb", [128, 128], BF16)
    ONES64 = din("i_ones64", [128, 64], BF16)
    ONES1 = din("i_ones1", [1, 128], BF16)
    QTHR = din("i_qthr", [128, 7])
    OUTX = dout("out", [Bb, T, C])

    QS_H = dtmp("qs_h", [Bb, NPAIR, 128, T], BF16)
    QS_L = dtmp("qs_l", [Bb, NPAIR, 128, T], BF16)
    KS_H = dtmp("ks_h", [Bb, NPAIR, 128, T], BF16)
    KS_L = dtmp("ks_l", [Bb, NPAIR, 128, T], BF16)
    VS_H = dtmp("vs_h", [Bb, NSC, 128, C], BF16)
    VS_L = dtmp("vs_l", [Bb, NSC, 128, C], BF16)
    AT_H = dtmp("at_h", [Bb, NPAIR, 128, T], BF16)
    AT_L = dtmp("at_l", [Bb, NPAIR, 128, T], BF16)

    cosT = nc.alloc_sbuf_tensor("cosT", [128, T], F32)
    sinT = nc.alloc_sbuf_tensor("sinT", [128, T], F32)
    permT = nc.alloc_sbuf_tensor("permT", [128, 128], F32)
    iden = nc.alloc_sbuf_tensor("iden", [128, 128], F32)
    idenb = nc.alloc_sbuf_tensor("idenb", [128, 128], BF16)
    ones64 = nc.alloc_sbuf_tensor("ones64", [128, 64], BF16)
    ones1 = nc.alloc_sbuf_tensor("ones1", [1, 128], BF16)
    bqkv_t = nc.alloc_sbuf_tensor("bqkv_t", [128, 24], F32)
    bqkv_sh = nc.alloc_sbuf_tensor("bqkv_sh", [128, 16], F32)
    qthr = nc.alloc_sbuf_tensor("qthr", [128, 7], F32)

    NMAIN = 48648
    main = nc.alloc_sbuf_tensor("main", [128, NMAIN], F32)

    class _Row:
        def __init__(self, off_b):
            self.off_b = off_b
        def ap(self):
            return main.ap().bitcast(BF16)[0:1, self.off_b // 2: self.off_b // 2 + 3 * C]
    o_browh = 4 * NMAIN - 2 * 3 * C * 2
    o_browl = o_browh + 3 * C * 2
    brow_h = _Row(o_browh)
    brow_l = _Row(o_browl)

    def f32(off_b, cols):
        assert off_b % 4 == 0 and off_b + 4 * cols <= 4 * NMAIN, (off_b, cols)
        return main.ap()[:, off_b // 4: off_b // 4 + cols]
    def bf16v(off_b, cols):
        assert off_b % 2 == 0 and off_b + 2 * cols <= 4 * NMAIN, (off_b, cols)
        return main.ap().bitcast(BF16)[:, off_b // 2: off_b // 2 + cols]

    o_xth, o_xtl = 0, 8 * T * 2
    o_wwh = o_xtl + 8 * T * 2
    o_wwl = o_wwh + 8 * C * 2
    o_wstg = o_wwl + 8 * C * 2
    o_xload = o_wstg + 2 * C * 4
    o_rt1 = o_xload + 2 * C * 4
    o_rt2 = o_rt1 + 2 * Tt * 4
    o_rt3 = o_rt2 + 2 * Tt * 4
    o_sth = o_rt3 + 2 * Tt * 4
    o_stl = o_sth + 2 * Tt * 2
    o_vth = o_stl + 2 * Tt * 2
    o_vtl = o_vth + 2 * C * 2
    o_rtq = o_vtl + 2 * C * 2
    p1_end = o_rtq + 2 * Tt * 4
    o_kgh, o_kgl = 0, PPG * T * 2
    o_vgh = o_kgl + PPG * T * 2
    o_vgl = o_vgh + NSC * 512 * 2
    o_qth = o_vgl + NSC * 512 * 2
    o_qtl = o_qth + PPG * Tt * 2
    o_e32 = o_qtl + PPG * Tt * 2
    o_el = o_e32 + 3 * 1024 * 4
    o_zr = o_el + 3 * 1024 * 2
    o_anf = o_zr + 2 * 512 * 4
    o_anh = o_anf + 2 * 512 * 4
    o_anl = o_anh + 2 * 512 * 2
    p2_end = o_anl + 2 * 512 * 2
    o_woh, o_wol = 0, 8 * C * 2
    o_wfh = o_wol + 8 * C * 2
    o_wfl = o_wfh + 8 * C * 2
    o_ath = o_wfl + 8 * C * 2
    o_atl = o_ath + 8 * Tt * 2
    o_x3 = o_atl + 8 * Tt * 2
    o_sres = o_x3 + 2 * C * 4
    o_x1t = o_sres + 2 * 2 * C * 4
    o_sq = o_x1t + 2 * C * 4
    o_qa = o_sq + 2 * C * 4
    o_qb = o_qa + 2 * C * 4
    o_qc = o_qb + 2 * C * 4
    o_qd = o_qc + 2 * C * 4
    o_st1 = o_qd + 2 * C * 4
    o_am = o_st1 + 16 * 4
    o_rsc = o_am + 64 * 4
    o_sc2 = o_rsc + 64 * 4
    p3_end = o_sc2 + 64 * 4
    o_xd4 = o_qb
    o_xdh = o_sq
    o_xdl = o_sq + 2 * C * 2
    o_o3 = o_qc
    o_x1t2 = o_qa
    o_wstg3 = o_x3
    o_xdtd = p3_end
    o_sres3 = o_xdtd + 2 * (2 * C * 2 + 2 * C * 2)
    assert o_sres3 + 2 * C * 4 <= o_browh, (o_sres3, o_browh)
    def sres_off(p3_):
        return o_sres + p3_ * 2 * C * 4 if p3_ < 2 else o_sres3
    assert max(p1_end, p2_end, p3_end) <= o_browh, (p1_end, p2_end, p3_end, o_browh)

    PSU = [nc._ctx.enter_context(nc.psum_tensor(f"ps{i}", [128, 1024], F32))
           for i in range(4)]

    p = Plan(nc)

    def dma(queue, sem, out_ap, in_ap, wait=()):
        p.emit(queue, lambda e, o=out_ap, i=in_ap: e.dma_start(out=o, in_=i),
               wait=wait, inc=[(sem, 16)])
        return p.tok(sem)

    def ve(fn, wait=(), drain=False):
        if drain:
            p.emit("vector", lambda e: (lambda r: (e.drain(), r)[1])(fn(e)),
                   wait=wait, inc=[("VE", 1)])
        else:
            p.emit("vector", fn, wait=wait, inc=[("VE", 1)])
        return p.tok("VE")

    def sca(fn, wait=(), drain=False):
        if drain:
            p.emit("scalar", lambda e: (lambda r: (e.drain(), r)[1])(fn(e)),
                   wait=wait, inc=[("ACT", 1)])
        else:
            p.emit("scalar", fn, wait=wait, inc=[("ACT", 1)])
        return p.tok("ACT")

    def gp(fn, wait=(), drain=False):
        if drain:
            p.emit("gpsimd", lambda e: (lambda r: (e.drain(), r)[1])(fn(e)),
                   wait=wait, inc=[("GPC", 1)])
        else:
            p.emit("gpsimd", fn, wait=wait, inc=[("GPC", 1)])
        return p.tok("GPC")

    def mm(fn, wait=()):
        p.emit("tensor", fn, wait=wait, inc=[("MM", 1)])
        return p.tok("MM")

    for dst, srcd in ((cosT, COS), (sinT, SIN), (permT, PERM), (iden, IDEN),
                      (idenb, IDENB), (ones64, ONES64), (ones1, ONES1),
                      (bqkv_t, BQKV_T), (bqkv_sh, BQKV_SH), (qthr, QTHR)):
        dma("sync", "LDC", dst.ap(), srcd)
    brow_stage = f32(o_wstg, 3 * C)[0:1, :]
    t_ld = dma("sync", "LDC", brow_stage, BROW)
    ve(lambda e: e.tensor_copy(brow_h.ap(), brow_stage), wait=[("LDC", t_ld)], drain=True)
    ve(lambda e: e.tensor_tensor(brow_l.ap(), brow_stage, brow_h.ap(), OP.subtract))

    def split_w_resident(src_dram, col0, ncols, o_h, o_l, tag):
        mm_guard = p.tok("MM")
        for cc in range(8):
            buf = cc % 2
            stg = f32(o_wstg + buf * C * 4, ncols)
            w = [("VE", split_w_resident.toks.get(buf, 0))]
            t_ld = dma("sync", f"LDW{buf}", stg,
                       src_dram[cc * 128:(cc + 1) * 128, col0:col0 + ncols], wait=w)
            hh = bf16v(o_h + cc * ncols * 2, ncols)
            ll = bf16v(o_l + cc * ncols * 2, ncols)
            ve(lambda e, hh=hh, stg=stg: e.tensor_copy(hh, stg),
               wait=[(f"LDW{buf}", t_ld), ("MM", mm_guard)], drain=True)
            tve = ve(lambda e, ll=ll, stg=stg, hh=hh:
                     e.tensor_tensor(ll, stg, hh, OP.subtract))
            split_w_resident.toks[buf] = tve
        return tve
    split_w_resident.toks = {}

    def phase1(b):
        p.barrier_all()
        if not hasattr(phase1, "stg_free"):
            phase1.stg_free = {0: 0, 1: 0}
            phase1.vst_free = {0: 0, 1: 0}
        w_first = split_w_resident(WQKV, 0, C, o_wwh, o_wwl, f"wq{b}")
        xload_free = {0: 0, 1: 0}
        bank_free = {0: 0, 1: 0, 2: 0, 3: 0}
        for tb in range(NTB):
            buf = tb % 2
            xl_ap = f32(o_xload + buf * C * 4, C)
            t_ld = dma("sync", f"LDX{buf}", xl_ap, X[b, tb * 128:(tb + 1) * 128, :],
                       wait=[("MM", xload_free[buf])])
            for half in range(2):
                bslot = (tb * 2 + half) % 4
                pbank = PSU[bslot // 2].ap()[:, (bslot % 2) * 512:(bslot % 2) * 512 + 512]
                for i4 in range(4):
                    cc = half * 4 + i4
                    t_mm = mm(lambda e, ps=pbank[:, i4 * 128:(i4 + 1) * 128],
                              xb=xl_ap[:, cc * 128:(cc + 1) * 128]:
                              e.transpose(ps, xb, iden.ap()),
                              wait=([(f"LDX{buf}", t_ld), ("VE", bank_free[bslot])]
                                    if i4 == 0 else ()))
                for i4 in range(4):
                    cc = half * 4 + i4
                    d_h = bf16v(o_xth + (cc * T + tb * 128) * 2, 128)
                    t_ac = sca(lambda e, d=d_h, s_=pbank[:, i4 * 128:(i4 + 1) * 128]:
                               e.activation(d, s_, AF.Copy),
                               wait=[("MM", t_mm)] if i4 == 0 else (),
                               drain=(i4 == 3))
                for i4 in range(4):
                    cc = half * 4 + i4
                    d_l = bf16v(o_xtl + (cc * T + tb * 128) * 2, 128)
                    h_ = bf16v(o_xth + (cc * T + tb * 128) * 2, 128)
                    tve = ve(lambda e, d=d_l, s_=pbank[:, i4 * 128:(i4 + 1) * 128], h=h_:
                             e.tensor_tensor(d, s_, h, OP.subtract),
                             wait=[("ACT", t_ac)] if i4 == 0 else ())
                bank_free[bslot] = tve
            xload_free[buf] = t_mm
        xbuild_ve = p.tok("VE")
        xbuild_act = p.tok("ACT")
        qk_slot_free = {i: 0 for i in range(6)}
        sh_free = 0
        stg_free = phase1.stg_free
        for wi, wave in enumerate(("q", "k")):
            if wi == 0:
                t_wsplit = w_first
            else:
                t_wsplit = split_w_resident(WQKV, wi * C, C, o_wwh, o_wwl, f"w{wave}{b}")
            for tt in range(NTT):
                for oc in range(8):
                    idx = tt * 8 + oc
                    slot = idx % 6
                    pbank = PSU[slot // 2].ap()[:, (slot % 2) * 512:(slot % 2) * 512 + Tt]
                    ops = []
                    for cc in range(8):
                        wh = bf16v(o_wwh + (cc * C + oc * 128) * 2, 128)
                        wl = bf16v(o_wwl + (cc * C + oc * 128) * 2, 128)
                        xh = bf16v(o_xth + (cc * T + tt * Tt) * 2, Tt)
                        xl = bf16v(o_xtl + (cc * T + tt * Tt) * 2, Tt)
                        ops += [(wh, xh), (wh, xl), (wl, xh)]
                    for i, (wa, xa) in enumerate(ops):
                        w0 = []
                        if i == 0:
                            w0 = [("VE", max(t_wsplit, qk_slot_free[slot], xbuild_ve)),
                                  ("ACT", xbuild_act)]
                        t_mm = mm(lambda e, pb=pbank, wa=wa, xa=xa,
                                  st=(i == 0), sp=(i == len(ops) - 1):
                                  e.matmul(pb, wa, xa, start=st, stop=sp), wait=w0)
                    occ = wi * 8 + oc
                    par = idx % 2
                    tq = f32(o_rtq + par * Tt * 4, Tt)
                    t1 = f32(o_rt1 + par * Tt * 4, Tt)
                    t2 = f32(o_rt2 + par * Tt * 4, Tt)
                    t3 = f32(o_rt3 + par * Tt * 4, Tt)
                    hh = bf16v(o_sth + par * Tt * 2, Tt)
                    ll = bf16v(o_stl + par * Tt * 2, Tt)
                    cosv = cosT.ap()[:, tt * Tt:(tt + 1) * Tt]
                    sinv = sinT.ap()[:, tt * Tt:(tt + 1) * Tt]
                    bias_ap = bqkv_t.ap()[:, occ:occ + 1]
                    tve1 = ve(lambda e, tq=tq, pb=pbank, ba=bias_ap:
                              e.tensor_scalar(tq, pb, ba, None, OP.add),
                              wait=[("MM", t_mm)], drain=True)
                    qk_slot_free[slot] = tve1
                    ve(lambda e, t1=t1, tq=tq, cv=cosv:
                       e.tensor_tensor(t1, tq, cv, OP.mult))
                    shslot = PSU[3].ap()[:, 512:512 + Tt]
                    t_mm2 = mm(lambda e, sh=shslot, tq=tq:
                               e.matmul(sh, permT.ap(), tq, start=True, stop=True),
                               wait=[("VE", max(tve1, sh_free))])
                    tve2 = ve(lambda e, t2=t2, sh=shslot, sv=sinv:
                              e.tensor_tensor(t2, sh, sv, OP.mult),
                              wait=[("MM", t_mm2)], drain=True)
                    sh_free = tve2
                    ve(lambda e, t3=t3, t1=t1, t2=t2: e.tensor_tensor(t3, t1, t2, OP.add),
                       drain=True)
                    t_ac = sca(lambda e, hh=hh, t3=t3: e.activation(hh, t3, AF.Copy),
                               wait=[(f"STR{par}", stg_free[par]), ("VE", p.tok("VE"))],
                               drain=True)
                    tve = ve(lambda e, ll=ll, t3=t3, hh=hh:
                             e.tensor_tensor(ll, t3, hh, OP.subtract),
                             wait=[("ACT", t_ac)])
                    DST_H = QS_H if wave == "q" else KS_H
                    DST_L = QS_L if wave == "q" else KS_L
                    dma("gpsimd", f"STR{par}", DST_H[b, oc, :, tt * Tt:(tt + 1) * Tt], hh,
                        wait=[("VE", tve)])
                    t_st = dma("gpsimd", f"STR{par}", DST_L[b, oc, :, tt * Tt:(tt + 1) * Tt], ll)
                    stg_free[par] = t_st
        t_wsplit = split_w_resident(WQKV, 2 * C, C, o_wwh, o_wwl, f"wv{b}")
        v_slot_free = {i: 0 for i in range(3)}
        vst_free = phase1.vst_free
        for sbk in range(NSC):
            slot = sbk % 3
            pb2 = PSU[slot].ap()
            for et in range(2):
                pbank = pb2[:, et * 512:(et + 1) * 512]
                ops = []
                for cc in range(8):
                    xh = bf16v(o_xth + (cc * T + sbk * 128) * 2, 128)
                    xl = bf16v(o_xtl + (cc * T + sbk * 128) * 2, 128)
                    wh = bf16v(o_wwh + (cc * C + et * 512) * 2, 512)
                    wl = bf16v(o_wwl + (cc * C + et * 512) * 2, 512)
                    ops += [(xh, wh), (xh, wl), (xl, wh)]
                for i, (sa, ma) in enumerate(ops):
                    w0 = ([("VE", max(t_wsplit, v_slot_free[slot])), ("ACT", xbuild_act)]
                          if i == 0 else [])
                    mm(lambda e, pb=pbank, sa=sa, ma=ma, st=(i == 0):
                       e.matmul(pb, sa, ma, start=st, stop=False), wait=w0)
                mm(lambda e, pb=pbank, et=et:
                   e.matmul(pb, ones1.ap(), brow_h.ap()[:, et * 512:(et + 1) * 512],
                            start=False, stop=False))
                t_mm = mm(lambda e, pb=pbank, et=et:
                          e.matmul(pb, ones1.ap(), brow_l.ap()[:, et * 512:(et + 1) * 512],
                                   start=False, stop=True))
            buf = sbk % 2
            vh = bf16v(o_vth + buf * C * 2, C)
            vl = bf16v(o_vtl + buf * C * 2, C)
            t_ac = sca(lambda e, vh=vh, pb=pb2: e.activation(vh, pb, AF.Copy),
                       wait=[("MM", t_mm), (f"STV{buf}", vst_free[buf])], drain=True)
            tve = ve(lambda e, vl=vl, pb=pb2, vh=vh:
                     e.tensor_tensor(vl, pb, vh, OP.subtract),
                     wait=[("ACT", t_ac)])
            v_slot_free[slot] = tve
            dma("gpsimd", f"STV{buf}", VS_H[b, sbk], vh, wait=[("VE", tve)])
            t_st = dma("gpsimd", f"STV{buf}", VS_L[b, sbk], vl)
            vst_free[buf] = t_st

    def phase2(b, g):
        p.barrier_all()
        t_k = {}
        for pp in range(PPG):
            pair = g * PPG + pp
            dma("sync", f"LDK{pp}", bf16v(o_kgh + pp * T * 2, T), KS_H[b, pair])
            t_k[pp] = dma("sync", f"LDK{pp}", bf16v(o_kgl + pp * T * 2, T),
                          KS_L[b, pair])
        t_v = {}
        for sbk in range(NSC):
            dma("sync", f"LDV{sbk}", bf16v(o_vgh + sbk * 512 * 2, 512),
                VS_H[b, sbk, :, g * 512:(g + 1) * 512])
            t_v[sbk] = dma("sync", f"LDV{sbk}", bf16v(o_vgl + sbk * 512 * 2, 512),
                           VS_L[b, sbk, :, g * 512:(g + 1) * 512])
        q_free = 0
        pv_free = {0: 0, 1: 0}
        an_free = {0: 0, 1: 0}
        exp_tok = {0: 0, 1: 0, 2: 0}
        spl_tok = {0: 0, 1: 0, 2: 0}
        spl_gp_tok = {0: 0, 1: 0, 2: 0}
        pvz_tok = {0: 0, 1: 0, 2: 0}
        for tt in range(NTT):
            for pp in range(PPG):
                dma("sync", "LDQ", bf16v(o_qth + pp * Tt * 2, Tt),
                    QS_H[b, g * PPG + pp, :, tt * Tt:(tt + 1) * Tt],
                    wait=[("MM", q_free)] if pp == 0 else ())
                t_q = dma("sync", "LDQ", bf16v(o_qtl + pp * Tt * 2, Tt),
                          QS_L[b, g * PPG + pp, :, tt * Tt:(tt + 1) * Tt])
            for pp in range(PPG):
                pvpar = 0
                pv_bank = PSU[3].ap()[:, 0:Tt]
                z_bank = PSU[3].ap()[:, 512:512 + Tt]
                qh = bf16v(o_qth + pp * Tt * 2, Tt)
                ql = bf16v(o_qtl + pp * Tt * 2, Tt)
                def s_chunk_mms(s):
                    par = s % 3
                    sps = PSU[par].ap()
                    kh = bf16v(o_kgh + (pp * T + s * 128) * 2, 128)
                    kl = bf16v(o_kgl + (pp * T + s * 128) * 2, 128)
                    ops = ((kh, qh), (kh, ql), (kl, qh))
                    for i, (ka, qa) in enumerate(ops):
                        for hi2 in range(2):
                            r0 = hi2 * 64
                            sbank = sps[:, hi2 * 512: hi2 * 512 + Tt]
                            tp = (hi2 * 64, 0)
                            w0 = []
                            if i == 0 and hi2 == 0:
                                w0 = [(f"LDK{pp}", t_k[pp]), ("LDQ", t_q),
                                      ("ACT", exp_tok[par])]
                            t_mm = mm(lambda e, sb_=sbank, ka=ka[r0:r0 + 64, :],
                                      qa=qa[r0:r0 + 64, :], st=(i == 0), sp=(i == 2), tp=tp:
                                      e.matmul(sb_, ka, qa, start=st, stop=sp,
                                               tile_position=tp), wait=w0)
                    return t_mm
                def s_chunk_tail(s, first_pv, last_pv):
                    par = s % 3
                    sps = PSU[par].ap()
                    e32 = f32(o_e32 + par * 1024 * 4, 2 * Tt)
                    e32b = main.ap().bitcast(BF16)
                    c0 = (o_e32 + par * 1024 * 4) // 2
                    eh = e32b[:, c0 + 1: c0 + 4 * Tt: 2]
                    el = bf16v(o_el + par * 1024 * 2, 2 * Tt)
                    if Tt == 512:
                        t_act = sca(lambda e, o=e32, i=sps:
                                    e.activation(o, i, AF.Exp, bias=0.0, scale=0.125),
                                    wait=[("MM", smm_tok[s]), ("VE", spl_tok[par]),
                                          ("GPC", spl_gp_tok[par])])
                    else:
                        sca(lambda e, o=e32[:, 0:Tt], i=sps[:, 0:Tt]:
                            e.activation(o, i, AF.Exp, bias=0.0, scale=0.125),
                            wait=[("MM", smm_tok[s]), ("VE", spl_tok[par]),
                                  ("GPC", spl_gp_tok[par])])
                        t_act = sca(lambda e, o=e32[:, Tt:2 * Tt], i=sps[:, 512:512 + Tt]:
                                    e.activation(o, i, AF.Exp, bias=0.0, scale=0.125))
                    exp_tok[par] = t_act
                    t_spl = ve(lambda e, el=el, e32=e32, eh=eh:
                               e.tensor_tensor(el, e32, eh, OP.subtract),
                               wait=[("ACT", t_act), ("MM", pvz_tok[par])])
                    spl_tok[par] = t_spl
                    wspl = ("VE", t_spl)
                    pvz = []
                    for hi2 in range(2):
                        ehh = eh[:, hi2 * Tt:(hi2 + 1) * Tt]
                        ell = el[:, hi2 * Tt:(hi2 + 1) * Tt]
                        vh_ = bf16v(o_vgh + (s * 512 + pp * 128 + hi2 * 64) * 2, 64)
                        vl_ = bf16v(o_vgl + (s * 512 + pp * 128 + hi2 * 64) * 2, 64)
                        tp = (0, hi2 * 64)
                        pvout = pv_bank[hi2 * 64: hi2 * 64 + 64, :]
                        zout = z_bank[hi2 * 64: hi2 * 64 + 64, :]
                        pvz.append((pvout, zout, tp,
                                    ((vh_, ehh), (vl_, ehh), (vh_, ell)), (ehh, ell)))
                    for i3 in range(3):
                        for hi2 in range(2):
                            pvout, zout, tp, pvops, zops = pvz[hi2]
                            va, ea = pvops[i3]
                            w0 = []
                            if i3 == 0 and hi2 == 0:
                                w0 = [("ACT", t_act)]
                                if first_pv:
                                    w0.append(("VE", pv_free[pvpar]))
                                if tt == 0 and pp == 0:
                                    w0.append((f"LDV{s}", t_v[s]))
                            if i3 == 2 and hi2 == 0:
                                w0 = [wspl]
                            mm(lambda e, o=pvout, va=va, ea=ea,
                               st=(first_pv and i3 == 0), sp=(last_pv and i3 == 2), tp=tp:
                               e.matmul(o, va, ea, start=st, stop=sp,
                                        tile_position=tp, skip_group_check=True), wait=w0)
                    for i2 in range(2):
                        for hi2 in range(2):
                            pvout, zout, tp, pvops, zops = pvz[hi2]
                            ea = zops[i2]
                            t_mm = mm(lambda e, o=zout, ea=ea,
                                      st=(first_pv and i2 == 0), sp=(last_pv and i2 == 1), tp=tp:
                                      e.matmul(o, ones64.ap(), ea, start=st,
                                               stop=sp, tile_position=tp,
                                               skip_group_check=True))
                    pvz_tok[par] = t_mm
                    return t_mm
                smm_tok = {}
                smm_tok[0] = s_chunk_mms(0)
                if NSC > 1:
                    smm_tok[1] = s_chunk_mms(1)
                for s in range(NSC):
                    if s + 2 < NSC:
                        smm_tok[s + 2] = s_chunk_mms(s + 2)
                    s_chunk_tail(s, first_pv=(s == 0), last_pv=(s == NSC - 1))
                t_mmz = p.tok("MM")
                zrv = f32(o_zr + pvpar * 512 * 4, Tt)
                anf = f32(o_anf + pvpar * 512 * 4, Tt)
                anh = bf16v(o_anh + pvpar * 512 * 2, Tt)
                anl = bf16v(o_anl + pvpar * 512 * 2, Tt)
                ve(lambda e, zr=zrv, zb=z_bank: e.reciprocal(zr, zb),
                   wait=[("MM", t_mmz), (f"STA{pvpar}", an_free[pvpar])], drain=True)
                tvm = ve(lambda e, o=anf, pv=pv_bank, zr=zrv:
                         e.tensor_tensor(o, pv, zr, OP.mult), drain=True)
                t_ac = sca(lambda e, anh=anh, anf=anf: e.activation(anh, anf, AF.Copy),
                           wait=[("VE", tvm)], drain=True)
                tve = ve(lambda e, anl=anl, anf=anf, anh=anh:
                         e.tensor_tensor(anl, anf, anh, OP.subtract),
                         wait=[("ACT", t_ac)])
                pv_free[pvpar] = tve
                pair = g * PPG + pp
                dma("gpsimd", f"STA{pvpar}", AT_H[b, pair, :, tt * Tt:(tt + 1) * Tt], anh,
                    wait=[("VE", tve)])
                t_st = dma("gpsimd", f"STA{pvpar}", AT_L[b, pair, :, tt * Tt:(tt + 1) * Tt], anl)
                an_free[pvpar] = t_st
            q_free = p.tok("MM")

    def quant_chain(src, dst, ncols, wait0=(), gp_ladder=False):
        nb = ncols // 32
        am = f32(o_am, nb); rsc = f32(o_rsc, nb); sc_ = f32(o_sc2, nb)
        sq = f32(o_sq, ncols); qb = f32(o_qb, ncols); qd = f32(o_qd, ncols)
        r3 = lambda a: a.rearrange("p (x y) -> p x y", x=nb)
        ve(lambda e: e.tensor_reduce(am, r3(src), AX.X, OP.max,
                                     apply_absolute_value=True), wait=wait0, drain=True)
        ve(lambda e: e.reciprocal(rsc, am), drain=True)
        ve(lambda e: e.tensor_scalar(rsc, rsc, 7.5, 1e30, OP.mult, OP.min), drain=True)
        ve(lambda e: e.reciprocal(sc_, rsc), drain=True)
        rb = rsc.unsqueeze(2).broadcast_to([128, nb, 32])
        scb = sc_.unsqueeze(2).broadcast_to([128, nb, 32])
        ve(lambda e: e.tensor_tensor(r3(qb), r3(src), rb, OP.mult), drain=True)
        ve(lambda e: e._custom_dve(OP_QINVS6, out=sq, in0=qb, s0=2.0, s1=4.0),
           drain=True)
        ve(lambda e: e.tensor_tensor(qd, qb, sq, OP.mult), drain=True)
        ve(lambda e: e.tensor_single_scalar(qb, qd, 16.0, OP.add), drain=True)
        for i, w in enumerate((32.0, 8.0, 2.0)):
            src_u = qb if i == 0 else qd
            ve(lambda e, w=w, s_=src_u, i=i: e._custom_dve(
                OP_LADDER2, out=qd, in0=s_, s0=w - 0.5, s1=w,
                imm2=w / 2.0 - 0.5, in1=qthr.ap()[:, 4 + i:5 + i]), drain=True)
        ve(lambda e: e.tensor_tensor(qd, qb, qd, OP.subtract), drain=True)
        ve(lambda e: e.tensor_scalar(qb, sq, 0.015625, -0.21875, OP.mult, OP.add),
           drain=True)
        ve(lambda e: e.tensor_tensor(sq, qb, sq, OP.mult), drain=True)
        ve(lambda e: e.tensor_single_scalar(sq, sq, 0.875, OP.add), drain=True)
        ve(lambda e: e.scalar_tensor_tensor(qd, qd, -16.0, sq, OP.add, OP.mult),
           drain=True)
        return ve(lambda e: e.tensor_tensor(r3(dst), r3(qd), scb, OP.mult),
                  drain=True)

    def layernorm2(srcs, dsts, jcols, dummies=None, wait_first=()):
        st = f32(o_st1, 16)
        nj = len(srcs)
        if dummies is None:
            dummies = [f32(o_sq, C)] * nj
        for j, (s_, jc) in enumerate(zip(srcs, jcols)):
            sca(lambda e, s_=s_, jc=jc, d_=dummies[j]:
                e.activation(d_, s_, AF.Square,
                             accum_out=st[:, 4 + jc:5 + jc]),
                wait=[("VE", p.tok("VE"))] + (list(wait_first) if j == 0 else []),
                drain=True)
        for j, (s_, jc) in enumerate(zip(srcs, jcols)):
            t_act = sca(lambda e, s_=s_, jc=jc, d_=dummies[j]:
                        e.activation(d_, s_, AF.Identity,
                                     accum_out=st[:, 0 + jc:1 + jc]), drain=True)
        p.emit("vector", lambda e: None, wait=[("ACT", t_act)])
        c_sum = st[:, 0:nj]; c_ssq = st[:, 4:4 + nj]
        c_mu = st[:, 8:8 + nj]; c_v = st[:, 12:12 + nj]
        ve(lambda e: e.tensor_single_scalar(c_mu, c_sum, 1.0 / C, OP.mult), drain=True)
        ve(lambda e: e.tensor_single_scalar(c_v, c_ssq, 1.0 / C, OP.mult), drain=True)
        ve(lambda e: e.tensor_tensor(c_sum, c_mu, c_mu, OP.mult), drain=True)
        ve(lambda e: e.tensor_tensor(c_v, c_v, c_sum, OP.subtract), drain=True)
        t0 = ve(lambda e: e.tensor_single_scalar(c_v, c_v, 1e-5, OP.add), drain=True)
        t_act = sca(lambda e: e.activation(c_ssq, c_v, AF.Sqrt), wait=[("VE", t0)],
                    drain=True)
        ve(lambda e: e.reciprocal(c_v, c_ssq), wait=[("ACT", t_act)], drain=True)
        tl = 0
        for j, (s_, d_, jc) in enumerate(zip(srcs, dsts, jcols)):
            tl = ve(lambda e, s_=s_, d_=d_, jc=jc:
                    e.tensor_scalar(d_, s_, st[:, 8 + jc:9 + jc], st[:, 12 + jc:13 + jc],
                                    OP.subtract, OP.mult), drain=(j == nj - 1))
        return tl

    def phase3():
        p.barrier_all()
        for cc in range(8):
            buf = cc % 2
            stg = f32(o_wstg3 + buf * C * 4, C)
            w = [("VE", phase3.wtok.get(buf, 0))]
            t_ld = dma("sync", f"LDW{buf}", stg, WOUT[cc * 128:(cc + 1) * 128, :], wait=w)
            hh = bf16v(o_woh + cc * C * 2, C)
            ll = bf16v(o_wol + cc * C * 2, C)
            ve(lambda e, hh=hh, stg=stg: e.tensor_copy(hh, stg),
               wait=[(f"LDW{buf}", t_ld)], drain=True)
            tve = ve(lambda e, ll=ll, stg=stg, hh=hh:
                     e.tensor_tensor(ll, stg, hh, OP.subtract))
            phase3.wtok[buf] = tve
        tsl_free = {0: 0, 1: 0}
        def psb16(sl):
            return PSU[3].ap().bitcast(BF16)[:, sl * 1024: sl * 1024 + 128]
        for obp in range(4):
            stg = f32(o_wstg3, 2 * C)
            stgv = stg.rearrange("p (a c) -> p a c", a=2)
            win = WFFN[2 * obp * 128:(2 * obp + 2) * 128, :].rearrange(
                "(a q) c -> q a c", q=128)
            t_ld = dma("sync", "LDW0", stgv, win,
                       wait=[("VE", max(phase3.wtok.get(0, 0), phase3.wtok.get(1, 0)))])
            xdq = f32(o_qb, 2 * C)
            tq = quant_chain(stg, xdq, 2 * C,
                             wait0=[("LDW0", t_ld),
                                    ("ACT", max(tsl_free.values()))])
            xdh = bf16v(o_xdh, 2 * C); xdl = bf16v(o_xdl, 2 * C)
            t_h = sca(lambda e, d=xdh, s_=xdq: e.activation(d, s_, AF.Copy),
                      wait=[("VE", tq)], drain=True)
            tve = ve(lambda e, d=xdl, s_=xdq, h=xdh:
                     e.tensor_tensor(d, s_, h, OP.subtract),
                     wait=[("ACT", t_h)], drain=True)
            phase3.wtok[0] = tve
            for j in range(2):
                ob = 2 * obp + j
                for half, srcv, o_dst in (("h", xdh, o_wfh), ("l", xdl, o_wfl)):
                    for cc in range(8):
                        sl = cc % 2
                        psl = psb16(sl)
                        t_mm = mm(lambda e, ps=psl,
                                  xb=srcv[:, j * C + cc * 128: j * C + (cc + 1) * 128]:
                                  e.transpose(ps, xb, idenb.ap()),
                                  wait=[("VE", tve), ("ACT", tsl_free[sl])])
                        d = bf16v(o_dst + (cc * C + ob * 128) * 2, 128)
                        last = (j == 1 and half == "l" and cc == 7)
                        t2 = sca(lambda e, d=d, s_=psl: e.activation(d, s_, AF.Copy),
                                 wait=[("MM", t_mm)], drain=last)
                        tsl_free[sl] = t2
        prep_guard = [("LDW0", p.tok("LDW0")), ("LDW1", p.tok("LDW1")),
                      ("VE", p.tok("VE"))]
        first_at = True
        x3_free = {0: 0, 1: 0}
        at_free = 0
        y_free = {0: 0, 1: 0}
        f_free = 0
        out_free = 0

        def stage1(b, tt, half, t_at, gidx):
            apar = gidx % 2
            atbase_h = o_ath + apar * (NPAIR * 256 * 2)
            atbase_l = o_atl + apar * (NPAIR * 256 * 2)
            if not hasattr(stage1, "first"):
                stage1.first = True
            """Wout 3-pass + bias + residual for 2 token blocks -> sres[par]."""
            nonlocal x3_free
            par = gidx % 2
            tb0 = tt * (Tt // 128) + half * 2
            t0_tok = tb0 * 128
            x3 = f32(o_x3, 2 * C)
            x3v = x3.rearrange("p (a c) -> p a c", a=2)
            xin = X[b, t0_tok:t0_tok + 256, :].rearrange("(a q) c -> q a c", q=128)
            wx = [("VE", max(x3_free.values()))]
            if stage1.first:
                wx += prep_guard
                stage1.first = False
            t_x = dma("sync", f"LDX{par}", x3v, xin, wait=wx)
            sres = f32(sres_off(gidx % 3), 2 * C)
            for j in range(2):
                tb4 = half * 2 + j
                ypar = j % 2
                ypsum = PSU[ypar].ap()
                for ot in range(2):
                    yb = ypsum[:, ot * 512:(ot + 1) * 512]
                    ops = []
                    for cc in range(8):
                        ah = bf16v(atbase_h + (cc * 256 + j * 128) * 2, 128)
                        al = bf16v(atbase_l + (cc * 256 + j * 128) * 2, 128)
                        wh = bf16v(o_woh + (cc * C + ot * 512) * 2, 512)
                        wl = bf16v(o_wol + (cc * C + ot * 512) * 2, 512)
                        ops += [(ah, wh), (ah, wl), (al, wh)]
                    for i, (sa, ma) in enumerate(ops):
                        w0 = []
                        if i == 0:
                            w0 = [(f"LDA{apar}", t_at[0]), (f"LAG{apar}", t_at[1]),
                                  ("VE", y_free[ypar])]
                        mm(lambda e, pb=yb, sa=sa, ma=ma, st=(i == 0):
                           e.matmul(pb, sa, ma, start=st, stop=False), wait=w0)
                    mm(lambda e, pb=yb, ot=ot:
                       e.matmul(pb, ones1.ap(),
                                brow_h.ap()[:, 2 * C + ot * 512: 2 * C + (ot + 1) * 512],
                                start=False, stop=False))
                    t_mm = mm(lambda e, pb=yb, ot=ot:
                              e.matmul(pb, ones1.ap(),
                                       brow_l.ap()[:, 2 * C + ot * 512: 2 * C + (ot + 1) * 512],
                                       start=False, stop=True))
                tve = ve(lambda e, s_=sres[:, j * C:(j + 1) * C], pb=ypsum,
                         x3j=x3[:, j * C:(j + 1) * C]:
                         e.tensor_tensor(s_, pb, x3j, OP.add),
                         wait=[("MM", t_mm), (f"LDX{par}", t_x)],
                         drain=(j == 1))
                y_free[ypar] = tve
            x3_free[par] = tve
            return sres

        def stage2a(b, tt, half, sres, gidx):
            x1t = f32(o_x1t if gidx % 2 == 0 else o_x1t2, 2 * C)
            layernorm2([sres[:, 0:C], sres[:, C:2 * C]],
                       [x1t[:, 0:C], x1t[:, C:2 * C]], [0, 1])
            xdq = f32(o_qb, 2 * C)
            tq = quant_chain(x1t, xdq, 2 * C, gp_ladder=GP_LADDER)
            xdh = bf16v(o_xdh, 2 * C); xdl = bf16v(o_xdl, 2 * C)
            t_h = sca(lambda e, d=xdh, s_=xdq: e.activation(d, s_, AF.Copy),
                      wait=[("VE", tq)], drain=True)
            tve = ve(lambda e, d=xdl, s_=xdq, h=xdh:
                     e.tensor_tensor(d, s_, h, OP.subtract),
                     wait=[("ACT", t_h)], drain=True)
            return x1t, tve, t_h

        def stage2b(b, tt, half, sres, gidx, x1t, tve, t_h):
            tpar = gidx % 2
            o_xdth = o_xdtd + tpar * 2 * (2 * C * 2)
            o_xdtl = o_xdth + 2 * C * 2
            xdh = bf16v(o_xdh, 2 * C); xdl = bf16v(o_xdl, 2 * C)
            tsl = {0: 0, 1: 0}
            def psb16_t(sl):
                return PSU[3].ap().bitcast(BF16)[:, sl * 1024: sl * 1024 + 128]
            for j in range(2):
                for hN, srcv, o_dst in (("h", xdh, o_xdth), ("l", xdl, o_xdtl)):
                    for cc in range(8):
                        sl = cc % 2
                        psl = psb16_t(sl)
                        wt = ([("ACT", max(t_h, tsl[sl]))] if hN == "h" else
                              [("VE", tve), ("ACT", tsl[sl])])
                        t_mm = mm(lambda e, ps=psl,
                                  xb=srcv[:, j * C + cc * 128: j * C + (cc + 1) * 128]:
                                  e.transpose(ps, xb, idenb.ap()),
                                  wait=wt)
                        d = bf16v(o_dst + (j * C + cc * 128) * 2, 128)
                        last = (j == 1 and hN == "l" and cc == 7)
                        t2 = sca(lambda e, d=d, s_=psl: e.activation(d, s_, AF.Copy),
                                 wait=[("MM", t_mm)], drain=last)
                        tsl[sl] = t2
            return p.tok("ACT")

        def stage3(b, tt, half, sres, gidx, x1t, t_xdt):
            nonlocal f_free, out_free
            tpar = gidx % 2
            o_xdth = o_xdtd + tpar * 2 * (2 * C * 2)
            o_xdtl = o_xdth + 2 * C * 2
            tb0 = tt * (Tt // 128) + half * 2
            t0_tok = tb0 * 128
            o3 = f32(o_o3, 2 * C)
            for j in range(2):
                fpsum = PSU[2].ap()
                for ot in range(2):
                    fb = fpsum[:, ot * 512:(ot + 1) * 512]
                    ops = []
                    for cc in range(8):
                        xh = bf16v(o_xdth + (j * C + cc * 128) * 2, 128)
                        xl = bf16v(o_xdtl + (j * C + cc * 128) * 2, 128)
                        wh = bf16v(o_wfh + (cc * C + ot * 512) * 2, 512)
                        wl = bf16v(o_wfl + (cc * C + ot * 512) * 2, 512)
                        ops += [(xh, wh), (xh, wl), (xl, wh)]
                    for i, (sa, ma) in enumerate(ops):
                        w0 = []
                        if i == 0:
                            w0 = [("ACT", t_xdt), ("VE", f_free)]
                        mm(lambda e, pb=fb, sa=sa, ma=ma, st=(i == 0):
                           e.matmul(pb, sa, ma, start=st, stop=False), wait=w0)
                    mm(lambda e, pb=fb, ot=ot:
                       e.matmul(pb, ones1.ap(),
                                brow_h.ap()[:, C + ot * 512: C + (ot + 1) * 512],
                                start=False, stop=False))
                    t_mmf = mm(lambda e, pb=fb, ot=ot:
                               e.matmul(pb, ones1.ap(),
                                        brow_l.ap()[:, C + ot * 512: C + (ot + 1) * 512],
                                        start=False, stop=True))
                tve = ve(lambda e, s_=sres[:, j * C:(j + 1) * C], pb=fpsum,
                         x1j=x1t[:, j * C:(j + 1) * C]:
                         e.tensor_tensor(s_, pb, x1j, OP.add),
                         wait=[("MM", t_mmf)],
                         drain=True)
                f_free = tve
            layernorm2([sres[:, 0:C], sres[:, C:2 * C]],
                       [o3[:, 0:C], o3[:, C:2 * C]], [0, 1],
                       dummies=[o3[:, 0:C], o3[:, C:2 * C]],
                       wait_first=[("STO0", out_free)])
            tvo = p.tok("VE")
            o3v = o3.rearrange("p (a c) -> p a c", a=2)
            oout = OUTX[b, t0_tok:t0_tok + 256, :].rearrange("(a q) c -> q a c", q=128)
            out_free = dma("gpsimd", "STO0", oout, o3v, wait=[("VE", tvo)])

        pend_s2 = None
        pend_s3 = None
        groups = [(b, tt, half) for b in range(Bb) for tt in range(NTT)
                  for half in range(Tt // 256)]
        at_mm_tok = {0: 0, 1: 0}
        at_ld_tok = {}

        def load_at(gidx, first=False):
            gb, gtt, ghalf = groups[gidx]
            apar = gidx % 2
            t0_tok = gtt * Tt + ghalf * 256
            w0 = [("MM", at_mm_tok[apar])] + (prep_guard if first else [])
            dsth = bf16v(o_ath + apar * (NPAIR * 256 * 2),
                         NPAIR * 256).rearrange("q (a c) -> q a c", a=NPAIR)
            dstl = bf16v(o_atl + apar * (NPAIR * 256 * 2),
                         NPAIR * 256).rearrange("q (a c) -> q a c", a=NPAIR)
            th = dma("sync", f"LDA{apar}", dsth,
                     AT_H[gb, :, :, t0_tok:t0_tok + 256].rearrange("a q c -> q a c"),
                     wait=w0)
            tl = dma("gpsimd", f"LAG{apar}", dstl,
                     AT_L[gb, :, :, t0_tok:t0_tok + 256].rearrange("a q c -> q a c"),
                     wait=list(w0))
            at_ld_tok[gidx] = (th, tl)

        load_at(0, first=True)
        for gidx, (b, tt, half) in enumerate(groups):
            if gidx + 1 < len(groups):
                load_at(gidx + 1)
            a_out = None
            if pend_s2 is not None:
                a_out = stage2a(*pend_s2)
            sres = stage1(b, tt, half, at_ld_tok[gidx], gidx)
            at_mm_tok[gidx % 2] = p.tok("MM")
            if pend_s3 is not None:
                stage3(*pend_s3)
            if pend_s2 is not None:
                t_xdt = stage2b(*pend_s2, *a_out)
                pend_s3 = pend_s2 + (a_out[0], t_xdt)
            pend_s2 = (b, tt, half, sres, gidx)
        if pend_s2 is not None:
            a_out = stage2a(*pend_s2)
            if pend_s3 is not None:
                stage3(*pend_s3)
            t_xdt = stage2b(*pend_s2, *a_out)
            stage3(*(pend_s2 + (a_out[0], t_xdt)))
        p.emit("gpsimd", lambda e: e.nop(),
               wait=[("STO0", p.tok("STO0"))])
    phase3.wtok = {}

    for b in range(Bb):
        phase1(b)
        for g in range(NG):
            phase2(b, g)
    phase3()

    with nc.Block() as block:
        @block.sync
        def _(eng):
            p.replay("sync", eng)
        @block.tensor
        def _(eng):
            p.replay("tensor", eng)
        @block.vector
        def _(eng):
            p.replay("vector", eng)
        @block.scalar
        def _(eng):
            p.replay("scalar", eng)
        @block.gpsimd
        def _(eng):
            p.replay("gpsimd", eng)
    mybir.codegen_inst_isa_subclasses(nc)
    nc._ctx.close()
    return nc




_T_FULL = 2048

def kernel(L0_attn_x, L0_attn_Wqkv, L0_attn_bqkv, L0_attn_Wout, L0_attn_bout,
           L0_ffn_weight, L0_ffn_bias):
    from concourse.bass_utils import run_bass_kernel_spmd
    x = np.ascontiguousarray(np.asarray(L0_attn_x, dtype=np.float32))
    Wqkv = np.ascontiguousarray(np.asarray(L0_attn_Wqkv, dtype=np.float32))
    bqkv = np.ascontiguousarray(np.asarray(L0_attn_bqkv, dtype=np.float32))
    Wout = np.ascontiguousarray(np.asarray(L0_attn_Wout, dtype=np.float32))
    bout = np.ascontiguousarray(np.asarray(L0_attn_bout, dtype=np.float32))
    Wffn = np.ascontiguousarray(np.asarray(L0_ffn_weight, dtype=np.float32))
    bffn = np.ascontiguousarray(np.asarray(L0_ffn_bias, dtype=np.float32))
    B_full, T, Cc = x.shape
    assert (B_full, T, Cc) == (16, _T_FULL, 1024), (x.shape,)
    n_cores = 8
    nc = build_core_program(T=T, Bb=B_full // n_cores)
    in_maps = []
    for c in range(n_cores):
        xs = x[c * 2:(c + 1) * 2]
        in_maps.append(host_inputs_for_core(xs, Wqkv, bqkv, Wout, bout,
                                            Wffn, bffn, T))
    res = run_bass_kernel_spmd(nc, in_maps, core_ids=list(range(n_cores)))
    out = np.concatenate([np.asarray(r["out"], dtype=np.float32)
                          for r in res.results], axis=0)
    return out


# version 21 (final)
# speedup vs baseline: 1.0368x; 1.0075x over previous
import contextlib
import numpy as np
import ml_dtypes
import concourse.bass as bass
import concourse.mybir as mybir
import concourse.dve_ops as dve_ops
from concourse.dve_op import DveOp
from concourse.dve_spec import (
    Spec, Src0, C0, C1, C2, C3, Zero, select, maxx, lower as dve_lower,
)
from concourse.dve_table_gen import dve_ver_for

F32 = mybir.dt.float32
BF16 = mybir.dt.bfloat16
OP = mybir.AluOpType
AF = mybir.ActivationFunctionType
AX = mybir.AxisListType
BF = ml_dtypes.bfloat16

C, H, Dh = 1024, 16, 64
NPAIR, NG = 8, 2
PPG = NPAIR // NG


def _register_custom_dve():
    if any(o.name == "QINVS6" for o in dve_ops.OPS):
        return
    ver = dve_ver_for("TRN2")
    a = maxx(Src0, Zero - Src0)
    c1 = a >= C0
    c2 = a >= C1
    spec_q = Spec(body=(C1 + C1) - (c1 * C1) - (c2 * C0))

    def ref_q(in0, in1, s0, s1, imm2):
        aa = np.abs(in0)
        cc1 = (aa >= s0).astype(in0.dtype)
        cc2 = (aa >= s1).astype(in0.dtype)
        return (s1 + s1) - cc1 * s1 - cc2 * s0

    u1 = select(Src0 >= C0, Src0 - C1, Src0)
    spec_l = Spec(body=select(u1 >= C2, u1 - C3, u1))

    def ref_l(in0, in1, s0, s1, imm2):
        w2 = np.asarray(in1)
        uu1 = np.where(in0 >= s0, in0 - s1, in0)
        return np.where(uu1 >= imm2, uu1 - w2, uu1)

    for name, spec, ref in (("QINVS6", spec_q, ref_q), ("LADDER2", spec_l, ref_l)):
        lowered = dve_lower(spec, ver=ver, name=name)
        op = DveOp(name=name, spec=spec, subdim=False,
                   uops_sha={ver: lowered.sha(ver)}, reference=ref)
        dve_ops.OPS.append(op)
        dve_ops._SUB_OPCODE_FOR_NAME[name] = (
            dve_ops._CUSTOM_DVE_ROW_BASE + len(dve_ops.OPS) - 1)
        dve_ops.CUSTOM_DVE_SPECS[name] = spec
        dve_ops.CUSTOM_DVE_REFS[name] = ref


_register_custom_dve()
OP_QINVS6 = next(o for o in dve_ops.OPS if o.name == "QINVS6")
OP_LADDER2 = next(o for o in dve_ops.OPS if o.name == "LADDER2")


class Plan:
    ENGINES = ("sync", "tensor", "vector", "scalar", "gpsimd")
    def __init__(self, nc):
        self.nc = nc
        self.steps = []
        self.sems = {}
        self.cnt = {}
        self.kbar = 0
        self.dma_sems = {}
        for s in ("MM", "ACT", "VE", "GPC", "PH"):
            self.sem(s)
    def sem(self, name):
        if name not in self.sems:
            self.sems[name] = self.nc._ctx.enter_context(self.nc.semaphore(name))
            self.cnt[name] = 0
        return self.sems[name]
    def tok(self, name):
        self.sem(name)
        return self.cnt[name]
    def emit(self, engine, fn, wait=(), inc=()):
        for s, _ in wait:
            self.sem(s)
        wait = [(s, v) for (s, v) in wait if v > 0]
        for s, n in inc:
            self.sem(s)
            self.cnt[s] += n
            if n == 16:
                self.dma_sems.setdefault(engine, set()).add(s)
        self.steps.append((engine, fn, tuple(wait), tuple(inc)))
    def barrier_all(self):
        self.kbar += 1
        for en in self.ENGINES:
            own = [(s, self.cnt[s]) for s in sorted(self.dma_sems.get(en, ()))
                   if self.cnt[s] > 0]
            self.cnt["PH"] += 1
            self.steps.append((en, ("PHINC",), tuple(own), ()))
        tgt = 5 * self.kbar
        for en in self.ENGINES:
            self.steps.append((en, lambda e: None, (("PH", tgt),), ()))
    def replay(self, engine, eng):
        for e, fn, waits, incs in self.steps:
            if e != engine:
                continue
            if isinstance(fn, tuple) and fn[0] == "PHINC":
                for s, v in waits:
                    eng.wait_ge(self.sems[s], v)
                eng.drain()
                eng.sem_inc(self.sems["PH"], 1)
                continue
            for s, v in waits:
                eng.wait_ge(self.sems[s], v)
            ins = fn(eng)
            for s, n in incs:
                assert ins is not None
                ins = ins.then_inc(self.sems[s], n)


def host_constants(T, D=64):
    inv_freq = 1.0 / (10000.0 ** (np.arange(0, D, 2, dtype=np.float64) / D))
    pos = np.arange(T, dtype=np.float64)
    freqs = pos[:, None] * inv_freq[None, :]
    emb = np.concatenate([freqs, freqs], axis=-1)
    sin = np.sin(emb).astype(np.float32)
    cos = np.cos(emb).astype(np.float32)
    cosT = np.tile(cos.T, (2, 1)).astype(np.float32)
    sgn = np.where(np.arange(64) < 32, -1.0, 1.0).astype(np.float32)
    sinT = np.tile((sin.T * sgn[:, None]), (2, 1)).astype(np.float32)
    permT = np.zeros((128, 128), dtype=np.float32)
    for blk in range(2):
        for j in range(64):
            s = 2 * j + 1 if j < 32 else 2 * (j - 32)
            permT[blk * 64 + s, blk * 64 + j] = 1.0
    iden = np.eye(128, dtype=np.float32)
    idenb = np.eye(128, dtype=BF)
    ones64 = np.ones((128, 64), dtype=BF)
    ones1 = np.ones((1, 128), dtype=BF)
    return dict(cosT=cosT, sinT=sinT, permT=permT, iden=iden, idenb=idenb,
                ones64=ones64, ones1=ones1)


def host_inputs_for_core(x_core, Wqkv, bqkv, Wout, bout, Wffn, bffn, T):
    cst = host_constants(T)
    bqkv_t = np.zeros((128, 24), dtype=np.float32)
    for oc in range(24):
        bqkv_t[:, oc] = bqkv[oc * 128:(oc + 1) * 128]
    src = np.array([2 * j + 1 if j < 32 else 2 * (j - 32) for j in range(64)])
    srcs = np.concatenate([src, src + 64])
    bqkv_sh = np.zeros((128, 16), dtype=np.float32)
    for oc in range(16):
        col = bqkv_t[:, oc]
        bqkv_sh[:, oc] = col[srcs]
    brow = np.zeros((1, 3 * C), dtype=np.float32)
    brow[0, 0:C] = bqkv[2 * C:3 * C]
    brow[0, C:2 * C] = bffn
    brow[0, 2 * C:3 * C] = bout
    return {
        "x": np.ascontiguousarray(x_core),
        "wqkv": np.ascontiguousarray(Wqkv), "wout": np.ascontiguousarray(Wout),
        "wffn": np.ascontiguousarray(Wffn),
        "i_bqkv_t": bqkv_t, "i_bqkv_sh": bqkv_sh, "brow": brow,
        "i_cosT": cst["cosT"], "i_sinT": cst["sinT"], "i_permT": cst["permT"],
        "i_iden": cst["iden"], "i_idenb": cst["idenb"],
        "i_ones64": cst["ones64"], "i_ones1": cst["ones1"],
        "i_qthr": np.tile(np.array([[-2.0, -4.0, -31.5, 0.0, 16.0, 4.0, 1.0]],
                                   np.float32), (128, 1)),
    }


GP_LADDER = False


def build_core_program(T=2048, Bb=2):
    Tt = min(T, 512)
    NTT = T // Tt
    NSC = T // 128
    NTB = T // 128

    nc = bass.Bass("TRN2", target_bir_lowering=False)
    nc._ctx = contextlib.ExitStack()

    def din(name, shape, dt=F32):
        return nc.dram_tensor(name, shape, dt, kind="ExternalInput").ap()
    def dout(name, shape, dt=F32):
        return nc.dram_tensor(name, shape, dt, kind="ExternalOutput").ap()
    def dtmp(name, shape, dt):
        return nc.dram_tensor(name, shape, dt, kind="Internal").ap()

    X = din("x", [Bb, T, C])
    WQKV = din("wqkv", [C, 3 * C])
    WOUT = din("wout", [C, C])
    WFFN = din("wffn", [C, C])
    BQKV_T = din("i_bqkv_t", [128, 24])
    BQKV_SH = din("i_bqkv_sh", [128, 16])
    BROW = din("brow", [1, 3 * C])
    COS = din("i_cosT", [128, T])
    SIN = din("i_sinT", [128, T])
    PERM = din("i_permT", [128, 128])
    IDEN = din("i_iden", [128, 128])
    IDENB = din("i_idenb", [128, 128], BF16)
    ONES64 = din("i_ones64", [128, 64], BF16)
    ONES1 = din("i_ones1", [1, 128], BF16)
    QTHR = din("i_qthr", [128, 7])
    OUTX = dout("out", [Bb, T, C])

    QS_H = dtmp("qs_h", [Bb, NPAIR, 128, T], BF16)
    QS_L = dtmp("qs_l", [Bb, NPAIR, 128, T], BF16)
    KS_H = dtmp("ks_h", [Bb, NPAIR, 128, T], BF16)
    KS_L = dtmp("ks_l", [Bb, NPAIR, 128, T], BF16)
    VS_H = dtmp("vs_h", [Bb, NSC, 128, C], BF16)
    VS_L = dtmp("vs_l", [Bb, NSC, 128, C], BF16)
    AT_H = dtmp("at_h", [Bb, NPAIR, 128, T], BF16)
    AT_L = dtmp("at_l", [Bb, NPAIR, 128, T], BF16)

    cosT = nc.alloc_sbuf_tensor("cosT", [128, T], F32)
    sinT = nc.alloc_sbuf_tensor("sinT", [128, T], F32)
    permT = nc.alloc_sbuf_tensor("permT", [128, 128], F32)
    iden = nc.alloc_sbuf_tensor("iden", [128, 128], F32)
    idenb = nc.alloc_sbuf_tensor("idenb", [128, 128], BF16)
    ones64 = nc.alloc_sbuf_tensor("ones64", [128, 64], BF16)
    ones1 = nc.alloc_sbuf_tensor("ones1", [1, 128], BF16)
    bqkv_t = nc.alloc_sbuf_tensor("bqkv_t", [128, 24], F32)
    bqkv_sh = nc.alloc_sbuf_tensor("bqkv_sh", [128, 16], F32)
    qthr = nc.alloc_sbuf_tensor("qthr", [128, 7], F32)

    NMAIN = 48648
    main = nc.alloc_sbuf_tensor("main", [128, NMAIN], F32)

    class _Row:
        def __init__(self, off_b):
            self.off_b = off_b
        def ap(self):
            return main.ap().bitcast(BF16)[0:1, self.off_b // 2: self.off_b // 2 + 3 * C]
    o_browh = 4 * NMAIN - 2 * 3 * C * 2
    o_browl = o_browh + 3 * C * 2
    brow_h = _Row(o_browh)
    brow_l = _Row(o_browl)

    def f32(off_b, cols):
        assert off_b % 4 == 0 and off_b + 4 * cols <= 4 * NMAIN, (off_b, cols)
        return main.ap()[:, off_b // 4: off_b // 4 + cols]
    def bf16v(off_b, cols):
        assert off_b % 2 == 0 and off_b + 2 * cols <= 4 * NMAIN, (off_b, cols)
        return main.ap().bitcast(BF16)[:, off_b // 2: off_b // 2 + cols]

    o_xth, o_xtl = 0, 8 * T * 2
    o_wwh = o_xtl + 8 * T * 2
    o_wwl = o_wwh + 8 * C * 2
    o_wstg = o_wwl + 8 * C * 2
    o_xload = o_wstg + 2 * C * 4
    o_rt1 = o_xload + 2 * C * 4
    o_rt2 = o_rt1 + 2 * Tt * 4
    o_rt3 = o_rt2 + 2 * Tt * 4
    o_sth = o_rt3 + 2 * Tt * 4
    o_stl = o_sth + 2 * Tt * 2
    o_vth = o_stl + 2 * Tt * 2
    o_vtl = o_vth + 2 * C * 2
    o_rtq = o_vtl + 2 * C * 2
    p1_end = o_rtq + 2 * Tt * 4
    o_kgh, o_kgl = 0, PPG * T * 2
    o_vgh = o_kgl + PPG * T * 2
    o_vgl = o_vgh + NSC * 512 * 2
    o_qth = o_vgl + NSC * 512 * 2
    o_qtl = o_qth + PPG * Tt * 2
    o_e32 = o_qtl + PPG * Tt * 2
    o_el = o_e32 + 3 * 1024 * 4
    o_zr = o_el + 3 * 1024 * 2
    o_anf = o_zr + 2 * 512 * 4
    o_anh = o_anf + 2 * 512 * 4
    o_anl = o_anh + 2 * 512 * 2
    p2_end = o_anl + 2 * 512 * 2
    o_woh, o_wol = 0, 8 * C * 2
    o_wfh = o_wol + 8 * C * 2
    o_wfl = o_wfh + 8 * C * 2
    o_ath = o_wfl + 8 * C * 2
    o_atl = o_ath + 8 * Tt * 2
    o_x3 = o_atl + 8 * Tt * 2
    o_sres = o_x3 + 2 * C * 4
    o_x1t = o_sres + 2 * 2 * C * 4
    o_sq = o_x1t + 2 * C * 4
    o_qa = o_sq + 2 * C * 4
    o_qb = o_qa + 2 * C * 4
    o_qc = o_qb + 2 * C * 4
    o_qd = o_qc + 2 * C * 4
    o_st1 = o_qd + 2 * C * 4
    o_am = o_st1 + 16 * 4
    o_rsc = o_am + 64 * 4
    o_sc2 = o_rsc + 64 * 4
    p3_end = o_sc2 + 64 * 4
    o_xd4 = o_qb
    o_xdh = o_sq
    o_xdl = o_sq + 2 * C * 2
    o_o3 = o_qc
    o_x1t2 = o_qa
    o_wstg3 = o_x3
    o_xdtd = p3_end
    o_sres3 = o_xdtd + 2 * (2 * C * 2 + 2 * C * 2)
    assert o_sres3 + 2 * C * 4 <= o_browh, (o_sres3, o_browh)
    def sres_off(p3_):
        return o_sres + p3_ * 2 * C * 4 if p3_ < 2 else o_sres3
    assert max(p1_end, p2_end, p3_end) <= o_browh, (p1_end, p2_end, p3_end, o_browh)

    PSU = [nc._ctx.enter_context(nc.psum_tensor(f"ps{i}", [128, 1024], F32))
           for i in range(4)]

    p = Plan(nc)

    def dma(queue, sem, out_ap, in_ap, wait=()):
        p.emit(queue, lambda e, o=out_ap, i=in_ap: e.dma_start(out=o, in_=i),
               wait=wait, inc=[(sem, 16)])
        return p.tok(sem)

    def ve(fn, wait=(), drain=False):
        if drain:
            p.emit("vector", lambda e: (lambda r: (e.drain(), r)[1])(fn(e)),
                   wait=wait, inc=[("VE", 1)])
        else:
            p.emit("vector", fn, wait=wait, inc=[("VE", 1)])
        return p.tok("VE")

    def sca(fn, wait=(), drain=False):
        if drain:
            p.emit("scalar", lambda e: (lambda r: (e.drain(), r)[1])(fn(e)),
                   wait=wait, inc=[("ACT", 1)])
        else:
            p.emit("scalar", fn, wait=wait, inc=[("ACT", 1)])
        return p.tok("ACT")

    def gp(fn, wait=(), drain=False):
        if drain:
            p.emit("gpsimd", lambda e: (lambda r: (e.drain(), r)[1])(fn(e)),
                   wait=wait, inc=[("GPC", 1)])
        else:
            p.emit("gpsimd", fn, wait=wait, inc=[("GPC", 1)])
        return p.tok("GPC")

    def mm(fn, wait=()):
        p.emit("tensor", fn, wait=wait, inc=[("MM", 1)])
        return p.tok("MM")

    for dst, srcd in ((cosT, COS), (sinT, SIN), (permT, PERM), (iden, IDEN),
                      (idenb, IDENB), (ones64, ONES64), (ones1, ONES1),
                      (bqkv_t, BQKV_T), (bqkv_sh, BQKV_SH), (qthr, QTHR)):
        dma("sync", "LDC", dst.ap(), srcd)
    brow_stage = f32(o_wstg, 3 * C)[0:1, :]
    t_ld = dma("sync", "LDC", brow_stage, BROW)
    ve(lambda e: e.tensor_copy(brow_h.ap(), brow_stage), wait=[("LDC", t_ld)], drain=True)
    ve(lambda e: e.tensor_tensor(brow_l.ap(), brow_stage, brow_h.ap(), OP.subtract))

    def split_w_resident(src_dram, col0, ncols, o_h, o_l, tag):
        mm_guard = p.tok("MM")
        for cc in range(8):
            buf = cc % 2
            stg = f32(o_wstg + buf * C * 4, ncols)
            w = [("VE", split_w_resident.toks.get(buf, 0))]
            t_ld = dma("sync", f"LDW{buf}", stg,
                       src_dram[cc * 128:(cc + 1) * 128, col0:col0 + ncols], wait=w)
            hh = bf16v(o_h + cc * ncols * 2, ncols)
            ll = bf16v(o_l + cc * ncols * 2, ncols)
            ve(lambda e, hh=hh, stg=stg: e.tensor_copy(hh, stg),
               wait=[(f"LDW{buf}", t_ld), ("MM", mm_guard)], drain=True)
            tve = ve(lambda e, ll=ll, stg=stg, hh=hh:
                     e.tensor_tensor(ll, stg, hh, OP.subtract))
            split_w_resident.toks[buf] = tve
        return tve
    split_w_resident.toks = {}

    def phase1(b):
        p.barrier_all()
        if not hasattr(phase1, "stg_free"):
            phase1.stg_free = {0: 0, 1: 0}
            phase1.vst_free = {0: 0, 1: 0}
        w_first = split_w_resident(WQKV, 0, C, o_wwh, o_wwl, f"wq{b}")
        xload_free = {0: 0, 1: 0}
        bank_free = {0: 0, 1: 0, 2: 0, 3: 0}
        for tb in range(NTB):
            buf = tb % 2
            xl_ap = f32(o_xload + buf * C * 4, C)
            t_ld = dma("sync", f"LDX{buf}", xl_ap, X[b, tb * 128:(tb + 1) * 128, :],
                       wait=[("MM", xload_free[buf])])
            for half in range(2):
                bslot = (tb * 2 + half) % 4
                pbank = PSU[bslot // 2].ap()[:, (bslot % 2) * 512:(bslot % 2) * 512 + 512]
                for i4 in range(4):
                    cc = half * 4 + i4
                    t_mm = mm(lambda e, ps=pbank[:, i4 * 128:(i4 + 1) * 128],
                              xb=xl_ap[:, cc * 128:(cc + 1) * 128]:
                              e.transpose(ps, xb, iden.ap()),
                              wait=([(f"LDX{buf}", t_ld), ("VE", bank_free[bslot])]
                                    if i4 == 0 else ()))
                for i4 in range(4):
                    cc = half * 4 + i4
                    d_h = bf16v(o_xth + (cc * T + tb * 128) * 2, 128)
                    t_ac = sca(lambda e, d=d_h, s_=pbank[:, i4 * 128:(i4 + 1) * 128]:
                               e.activation(d, s_, AF.Copy),
                               wait=[("MM", t_mm)] if i4 == 0 else (),
                               drain=(i4 == 3))
                for i4 in range(4):
                    cc = half * 4 + i4
                    d_l = bf16v(o_xtl + (cc * T + tb * 128) * 2, 128)
                    h_ = bf16v(o_xth + (cc * T + tb * 128) * 2, 128)
                    tve = ve(lambda e, d=d_l, s_=pbank[:, i4 * 128:(i4 + 1) * 128], h=h_:
                             e.tensor_tensor(d, s_, h, OP.subtract),
                             wait=[("ACT", t_ac)] if i4 == 0 else ())
                bank_free[bslot] = tve
            xload_free[buf] = t_mm
        xbuild_ve = p.tok("VE")
        xbuild_act = p.tok("ACT")
        qk_slot_free = {i: 0 for i in range(6)}
        sh_free = 0
        stg_free = phase1.stg_free
        for wi, wave in enumerate(("q", "k")):
            if wi == 0:
                t_wsplit = w_first
            else:
                t_wsplit = split_w_resident(WQKV, wi * C, C, o_wwh, o_wwl, f"w{wave}{b}")
            for tt in range(NTT):
                for oc in range(8):
                    idx = tt * 8 + oc
                    slot = idx % 6
                    pbank = PSU[slot // 2].ap()[:, (slot % 2) * 512:(slot % 2) * 512 + Tt]
                    ops = []
                    for cc in range(8):
                        wh = bf16v(o_wwh + (cc * C + oc * 128) * 2, 128)
                        wl = bf16v(o_wwl + (cc * C + oc * 128) * 2, 128)
                        xh = bf16v(o_xth + (cc * T + tt * Tt) * 2, Tt)
                        xl = bf16v(o_xtl + (cc * T + tt * Tt) * 2, Tt)
                        ops += [(wh, xh), (wh, xl), (wl, xh)]
                    for i, (wa, xa) in enumerate(ops):
                        w0 = []
                        if i == 0:
                            w0 = [("VE", max(t_wsplit, qk_slot_free[slot], xbuild_ve)),
                                  ("ACT", xbuild_act)]
                        t_mm = mm(lambda e, pb=pbank, wa=wa, xa=xa,
                                  st=(i == 0), sp=(i == len(ops) - 1):
                                  e.matmul(pb, wa, xa, start=st, stop=sp), wait=w0)
                    occ = wi * 8 + oc
                    par = idx % 2
                    tq = f32(o_rtq + par * Tt * 4, Tt)
                    t1 = f32(o_rt1 + par * Tt * 4, Tt)
                    t2 = f32(o_rt2 + par * Tt * 4, Tt)
                    t3 = f32(o_rt3 + par * Tt * 4, Tt)
                    hh = bf16v(o_sth + par * Tt * 2, Tt)
                    ll = bf16v(o_stl + par * Tt * 2, Tt)
                    cosv = cosT.ap()[:, tt * Tt:(tt + 1) * Tt]
                    sinv = sinT.ap()[:, tt * Tt:(tt + 1) * Tt]
                    bias_ap = bqkv_t.ap()[:, occ:occ + 1]
                    tve1 = ve(lambda e, tq=tq, pb=pbank, ba=bias_ap:
                              e.tensor_scalar(tq, pb, ba, None, OP.add),
                              wait=[("MM", t_mm)], drain=True)
                    qk_slot_free[slot] = tve1
                    ve(lambda e, t1=t1, tq=tq, cv=cosv:
                       e.tensor_tensor(t1, tq, cv, OP.mult))
                    shslot = PSU[3].ap()[:, 512:512 + Tt]
                    t_mm2 = mm(lambda e, sh=shslot, tq=tq:
                               e.matmul(sh, permT.ap(), tq, start=True, stop=True),
                               wait=[("VE", max(tve1, sh_free))])
                    tve2 = ve(lambda e, t2=t2, sh=shslot, sv=sinv:
                              e.tensor_tensor(t2, sh, sv, OP.mult),
                              wait=[("MM", t_mm2)], drain=True)
                    sh_free = tve2
                    ve(lambda e, t3=t3, t1=t1, t2=t2: e.tensor_tensor(t3, t1, t2, OP.add),
                       drain=True)
                    t_ac = sca(lambda e, hh=hh, t3=t3: e.activation(hh, t3, AF.Copy),
                               wait=[(f"STR{par}", stg_free[par]), ("VE", p.tok("VE"))],
                               drain=True)
                    tve = ve(lambda e, ll=ll, t3=t3, hh=hh:
                             e.tensor_tensor(ll, t3, hh, OP.subtract),
                             wait=[("ACT", t_ac)])
                    DST_H = QS_H if wave == "q" else KS_H
                    DST_L = QS_L if wave == "q" else KS_L
                    dma("gpsimd", f"STR{par}", DST_H[b, oc, :, tt * Tt:(tt + 1) * Tt], hh,
                        wait=[("VE", tve)])
                    t_st = dma("gpsimd", f"STR{par}", DST_L[b, oc, :, tt * Tt:(tt + 1) * Tt], ll)
                    stg_free[par] = t_st
        t_wsplit = split_w_resident(WQKV, 2 * C, C, o_wwh, o_wwl, f"wv{b}")
        v_slot_free = {i: 0 for i in range(3)}
        vst_free = phase1.vst_free
        for sbk in range(NSC):
            slot = sbk % 3
            pb2 = PSU[slot].ap()
            for et in range(2):
                pbank = pb2[:, et * 512:(et + 1) * 512]
                ops = []
                for cc in range(8):
                    xh = bf16v(o_xth + (cc * T + sbk * 128) * 2, 128)
                    xl = bf16v(o_xtl + (cc * T + sbk * 128) * 2, 128)
                    wh = bf16v(o_wwh + (cc * C + et * 512) * 2, 512)
                    wl = bf16v(o_wwl + (cc * C + et * 512) * 2, 512)
                    ops += [(xh, wh), (xh, wl), (xl, wh)]
                for i, (sa, ma) in enumerate(ops):
                    w0 = ([("VE", max(t_wsplit, v_slot_free[slot])), ("ACT", xbuild_act)]
                          if i == 0 else [])
                    mm(lambda e, pb=pbank, sa=sa, ma=ma, st=(i == 0):
                       e.matmul(pb, sa, ma, start=st, stop=False), wait=w0)
                mm(lambda e, pb=pbank, et=et:
                   e.matmul(pb, ones1.ap(), brow_h.ap()[:, et * 512:(et + 1) * 512],
                            start=False, stop=False))
                t_mm = mm(lambda e, pb=pbank, et=et:
                          e.matmul(pb, ones1.ap(), brow_l.ap()[:, et * 512:(et + 1) * 512],
                                   start=False, stop=True))
            buf = sbk % 2
            vh = bf16v(o_vth + buf * C * 2, C)
            vl = bf16v(o_vtl + buf * C * 2, C)
            t_ac = sca(lambda e, vh=vh, pb=pb2: e.activation(vh, pb, AF.Copy),
                       wait=[("MM", t_mm), (f"STV{buf}", vst_free[buf])], drain=True)
            tve = ve(lambda e, vl=vl, pb=pb2, vh=vh:
                     e.tensor_tensor(vl, pb, vh, OP.subtract),
                     wait=[("ACT", t_ac)])
            v_slot_free[slot] = tve
            dma("gpsimd", f"STV{buf}", VS_H[b, sbk], vh, wait=[("VE", tve)])
            t_st = dma("gpsimd", f"STV{buf}", VS_L[b, sbk], vl)
            vst_free[buf] = t_st

    def phase2(b, g):
        p.barrier_all()
        for pp in range(PPG):
            pair = g * PPG + pp
            dma("sync", "LDS", bf16v(o_kgh + pp * T * 2, T), KS_H[b, pair])
            dma("sync", "LDS", bf16v(o_kgl + pp * T * 2, T), KS_L[b, pair])
        for sbk in range(NSC):
            dma("sync", "LDS", bf16v(o_vgh + sbk * 512 * 2, 512),
                VS_H[b, sbk, :, g * 512:(g + 1) * 512])
            t_slab = dma("sync", "LDS", bf16v(o_vgl + sbk * 512 * 2, 512),
                         VS_L[b, sbk, :, g * 512:(g + 1) * 512])
        q_free = 0
        pv_free = {0: 0, 1: 0}
        an_free = {0: 0, 1: 0}
        exp_tok = {0: 0, 1: 0, 2: 0}
        spl_tok = {0: 0, 1: 0, 2: 0}
        spl_gp_tok = {0: 0, 1: 0, 2: 0}
        pvz_tok = {0: 0, 1: 0, 2: 0}
        for tt in range(NTT):
            for pp in range(PPG):
                dma("sync", "LDQ", bf16v(o_qth + pp * Tt * 2, Tt),
                    QS_H[b, g * PPG + pp, :, tt * Tt:(tt + 1) * Tt],
                    wait=[("MM", q_free)] if pp == 0 else ())
                t_q = dma("sync", "LDQ", bf16v(o_qtl + pp * Tt * 2, Tt),
                          QS_L[b, g * PPG + pp, :, tt * Tt:(tt + 1) * Tt])
            for pp in range(PPG):
                pvpar = 0
                pv_bank = PSU[3].ap()[:, 0:Tt]
                z_bank = PSU[3].ap()[:, 512:512 + Tt]
                qh = bf16v(o_qth + pp * Tt * 2, Tt)
                ql = bf16v(o_qtl + pp * Tt * 2, Tt)
                def s_chunk_mms(s):
                    par = s % 3
                    sps = PSU[par].ap()
                    kh = bf16v(o_kgh + (pp * T + s * 128) * 2, 128)
                    kl = bf16v(o_kgl + (pp * T + s * 128) * 2, 128)
                    ops = ((kh, qh), (kh, ql), (kl, qh))
                    for i, (ka, qa) in enumerate(ops):
                        for hi2 in range(2):
                            r0 = hi2 * 64
                            sbank = sps[:, hi2 * 512: hi2 * 512 + Tt]
                            tp = (hi2 * 64, 0)
                            w0 = []
                            if i == 0 and hi2 == 0:
                                w0 = [("LDS", t_slab), ("LDQ", t_q),
                                      ("ACT", exp_tok[par])]
                            t_mm = mm(lambda e, sb_=sbank, ka=ka[r0:r0 + 64, :],
                                      qa=qa[r0:r0 + 64, :], st=(i == 0), sp=(i == 2), tp=tp:
                                      e.matmul(sb_, ka, qa, start=st, stop=sp,
                                               tile_position=tp), wait=w0)
                    return t_mm
                def s_chunk_tail(s, first_pv, last_pv):
                    par = s % 3
                    sps = PSU[par].ap()
                    e32 = f32(o_e32 + par * 1024 * 4, 2 * Tt)
                    e32b = main.ap().bitcast(BF16)
                    c0 = (o_e32 + par * 1024 * 4) // 2
                    eh = e32b[:, c0 + 1: c0 + 4 * Tt: 2]
                    el = bf16v(o_el + par * 1024 * 2, 2 * Tt)
                    if Tt == 512:
                        t_act = sca(lambda e, o=e32, i=sps:
                                    e.activation(o, i, AF.Exp, bias=0.0, scale=0.125),
                                    wait=[("MM", smm_tok[s]), ("VE", spl_tok[par]),
                                          ("GPC", spl_gp_tok[par])])
                    else:
                        sca(lambda e, o=e32[:, 0:Tt], i=sps[:, 0:Tt]:
                            e.activation(o, i, AF.Exp, bias=0.0, scale=0.125),
                            wait=[("MM", smm_tok[s]), ("VE", spl_tok[par]),
                                  ("GPC", spl_gp_tok[par])])
                        t_act = sca(lambda e, o=e32[:, Tt:2 * Tt], i=sps[:, 512:512 + Tt]:
                                    e.activation(o, i, AF.Exp, bias=0.0, scale=0.125))
                    exp_tok[par] = t_act
                    t_spl = ve(lambda e, el=el, e32=e32, eh=eh:
                               e.tensor_tensor(el, e32, eh, OP.subtract),
                               wait=[("ACT", t_act), ("MM", pvz_tok[par])])
                    spl_tok[par] = t_spl
                    wspl = ("VE", t_spl)
                    pvz = []
                    for hi2 in range(2):
                        ehh = eh[:, hi2 * Tt:(hi2 + 1) * Tt]
                        ell = el[:, hi2 * Tt:(hi2 + 1) * Tt]
                        vh_ = bf16v(o_vgh + (s * 512 + pp * 128 + hi2 * 64) * 2, 64)
                        vl_ = bf16v(o_vgl + (s * 512 + pp * 128 + hi2 * 64) * 2, 64)
                        tp = (0, hi2 * 64)
                        pvout = pv_bank[hi2 * 64: hi2 * 64 + 64, :]
                        zout = z_bank[hi2 * 64: hi2 * 64 + 64, :]
                        pvz.append((pvout, zout, tp,
                                    ((vh_, ehh), (vl_, ehh), (vh_, ell)), (ehh, ell)))
                    for i3 in range(3):
                        for hi2 in range(2):
                            pvout, zout, tp, pvops, zops = pvz[hi2]
                            va, ea = pvops[i3]
                            w0 = []
                            if i3 == 0 and hi2 == 0:
                                w0 = [("ACT", t_act)]
                                if first_pv:
                                    w0.append(("VE", pv_free[pvpar]))
                            if i3 == 2 and hi2 == 0:
                                w0 = [wspl]
                            mm(lambda e, o=pvout, va=va, ea=ea,
                               st=(first_pv and i3 == 0), sp=(last_pv and i3 == 2), tp=tp:
                               e.matmul(o, va, ea, start=st, stop=sp,
                                        tile_position=tp, skip_group_check=True), wait=w0)
                    for i2 in range(2):
                        for hi2 in range(2):
                            pvout, zout, tp, pvops, zops = pvz[hi2]
                            ea = zops[i2]
                            t_mm = mm(lambda e, o=zout, ea=ea,
                                      st=(first_pv and i2 == 0), sp=(last_pv and i2 == 1), tp=tp:
                                      e.matmul(o, ones64.ap(), ea, start=st,
                                               stop=sp, tile_position=tp,
                                               skip_group_check=True))
                    pvz_tok[par] = t_mm
                    return t_mm
                smm_tok = {}
                smm_tok[0] = s_chunk_mms(0)
                if NSC > 1:
                    smm_tok[1] = s_chunk_mms(1)
                for s in range(NSC):
                    if s + 2 < NSC:
                        smm_tok[s + 2] = s_chunk_mms(s + 2)
                    s_chunk_tail(s, first_pv=(s == 0), last_pv=(s == NSC - 1))
                t_mmz = p.tok("MM")
                zrv = f32(o_zr + pvpar * 512 * 4, Tt)
                anf = f32(o_anf + pvpar * 512 * 4, Tt)
                anh = bf16v(o_anh + pvpar * 512 * 2, Tt)
                anl = bf16v(o_anl + pvpar * 512 * 2, Tt)
                ve(lambda e, zr=zrv, zb=z_bank: e.reciprocal(zr, zb),
                   wait=[("MM", t_mmz), (f"STA{pvpar}", an_free[pvpar])], drain=True)
                tvm = ve(lambda e, o=anf, pv=pv_bank, zr=zrv:
                         e.tensor_tensor(o, pv, zr, OP.mult), drain=True)
                t_ac = sca(lambda e, anh=anh, anf=anf: e.activation(anh, anf, AF.Copy),
                           wait=[("VE", tvm)], drain=True)
                tve = ve(lambda e, anl=anl, anf=anf, anh=anh:
                         e.tensor_tensor(anl, anf, anh, OP.subtract),
                         wait=[("ACT", t_ac)])
                pv_free[pvpar] = tve
                pair = g * PPG + pp
                dma("gpsimd", f"STA{pvpar}", AT_H[b, pair, :, tt * Tt:(tt + 1) * Tt], anh,
                    wait=[("VE", tve)])
                t_st = dma("gpsimd", f"STA{pvpar}", AT_L[b, pair, :, tt * Tt:(tt + 1) * Tt], anl)
                an_free[pvpar] = t_st
            q_free = p.tok("MM")

    def quant_chain(src, dst, ncols, wait0=(), gp_ladder=False):
        nb = ncols // 32
        am = f32(o_am, nb); rsc = f32(o_rsc, nb); sc_ = f32(o_sc2, nb)
        sq = f32(o_sq, ncols); qb = f32(o_qb, ncols); qd = f32(o_qd, ncols)
        r3 = lambda a: a.rearrange("p (x y) -> p x y", x=nb)
        ve(lambda e: e.tensor_reduce(am, r3(src), AX.X, OP.max,
                                     apply_absolute_value=True), wait=wait0, drain=True)
        ve(lambda e: e.reciprocal(rsc, am), drain=True)
        ve(lambda e: e.tensor_scalar(rsc, rsc, 7.5, 1e30, OP.mult, OP.min), drain=True)
        ve(lambda e: e.reciprocal(sc_, rsc), drain=True)
        rb = rsc.unsqueeze(2).broadcast_to([128, nb, 32])
        scb = sc_.unsqueeze(2).broadcast_to([128, nb, 32])
        ve(lambda e: e.tensor_tensor(r3(qb), r3(src), rb, OP.mult), drain=True)
        ve(lambda e: e._custom_dve(OP_QINVS6, out=sq, in0=qb, s0=2.0, s1=4.0),
           drain=True)
        ve(lambda e: e.tensor_tensor(qd, qb, sq, OP.mult), drain=True)
        ve(lambda e: e.tensor_single_scalar(qb, qd, 16.0, OP.add), drain=True)
        for i, w in enumerate((32.0, 8.0, 2.0)):
            src_u = qb if i == 0 else qd
            ve(lambda e, w=w, s_=src_u, i=i: e._custom_dve(
                OP_LADDER2, out=qd, in0=s_, s0=w - 0.5, s1=w,
                imm2=w / 2.0 - 0.5, in1=qthr.ap()[:, 4 + i:5 + i]), drain=True)
        ve(lambda e: e.tensor_tensor(qd, qb, qd, OP.subtract), drain=True)
        ve(lambda e: e.tensor_scalar(qb, sq, 0.015625, -0.21875, OP.mult, OP.add),
           drain=True)
        ve(lambda e: e.tensor_tensor(sq, qb, sq, OP.mult), drain=True)
        ve(lambda e: e.tensor_single_scalar(sq, sq, 0.875, OP.add), drain=True)
        ve(lambda e: e.scalar_tensor_tensor(qd, qd, -16.0, sq, OP.add, OP.mult),
           drain=True)
        return ve(lambda e: e.tensor_tensor(r3(dst), r3(qd), scb, OP.mult),
                  drain=True)

    def layernorm2(srcs, dsts, jcols, dummies=None, wait_first=()):
        st = f32(o_st1, 16)
        nj = len(srcs)
        if dummies is None:
            dummies = [f32(o_sq, C)] * nj
        for j, (s_, jc) in enumerate(zip(srcs, jcols)):
            sca(lambda e, s_=s_, jc=jc, d_=dummies[j]:
                e.activation(d_, s_, AF.Square,
                             accum_out=st[:, 4 + jc:5 + jc]),
                wait=[("VE", p.tok("VE"))] + (list(wait_first) if j == 0 else []),
                drain=True)
        for j, (s_, jc) in enumerate(zip(srcs, jcols)):
            t_act = sca(lambda e, s_=s_, jc=jc, d_=dummies[j]:
                        e.activation(d_, s_, AF.Identity,
                                     accum_out=st[:, 0 + jc:1 + jc]), drain=True)
        p.emit("vector", lambda e: None, wait=[("ACT", t_act)])
        c_sum = st[:, 0:nj]; c_ssq = st[:, 4:4 + nj]
        c_mu = st[:, 8:8 + nj]; c_v = st[:, 12:12 + nj]
        ve(lambda e: e.tensor_single_scalar(c_mu, c_sum, 1.0 / C, OP.mult), drain=True)
        ve(lambda e: e.tensor_single_scalar(c_v, c_ssq, 1.0 / C, OP.mult), drain=True)
        ve(lambda e: e.tensor_tensor(c_sum, c_mu, c_mu, OP.mult), drain=True)
        ve(lambda e: e.tensor_tensor(c_v, c_v, c_sum, OP.subtract), drain=True)
        t0 = ve(lambda e: e.tensor_single_scalar(c_v, c_v, 1e-5, OP.add), drain=True)
        t_act = sca(lambda e: e.activation(c_ssq, c_v, AF.Sqrt), wait=[("VE", t0)],
                    drain=True)
        ve(lambda e: e.reciprocal(c_v, c_ssq), wait=[("ACT", t_act)], drain=True)
        tl = 0
        for j, (s_, d_, jc) in enumerate(zip(srcs, dsts, jcols)):
            tl = ve(lambda e, s_=s_, d_=d_, jc=jc:
                    e.tensor_scalar(d_, s_, st[:, 8 + jc:9 + jc], st[:, 12 + jc:13 + jc],
                                    OP.subtract, OP.mult), drain=(j == nj - 1))
        return tl

    def phase3():
        p.barrier_all()
        for cc in range(8):
            buf = cc % 2
            stg = f32(o_wstg3 + buf * C * 4, C)
            w = [("VE", phase3.wtok.get(buf, 0))]
            t_ld = dma("sync", f"LDW{buf}", stg, WOUT[cc * 128:(cc + 1) * 128, :], wait=w)
            hh = bf16v(o_woh + cc * C * 2, C)
            ll = bf16v(o_wol + cc * C * 2, C)
            ve(lambda e, hh=hh, stg=stg: e.tensor_copy(hh, stg),
               wait=[(f"LDW{buf}", t_ld)], drain=True)
            tve = ve(lambda e, ll=ll, stg=stg, hh=hh:
                     e.tensor_tensor(ll, stg, hh, OP.subtract))
            phase3.wtok[buf] = tve
        tsl_free = {0: 0, 1: 0}
        def psb16(sl):
            return PSU[3].ap().bitcast(BF16)[:, sl * 1024: sl * 1024 + 128]
        for obp in range(4):
            stg = f32(o_wstg3, 2 * C)
            stgv = stg.rearrange("p (a c) -> p a c", a=2)
            win = WFFN[2 * obp * 128:(2 * obp + 2) * 128, :].rearrange(
                "(a q) c -> q a c", q=128)
            t_ld = dma("sync", "LDW0", stgv, win,
                       wait=[("VE", max(phase3.wtok.get(0, 0), phase3.wtok.get(1, 0)))])
            xdq = f32(o_qb, 2 * C)
            tq = quant_chain(stg, xdq, 2 * C,
                             wait0=[("LDW0", t_ld),
                                    ("ACT", max(tsl_free.values()))])
            xdh = bf16v(o_xdh, 2 * C); xdl = bf16v(o_xdl, 2 * C)
            t_h = sca(lambda e, d=xdh, s_=xdq: e.activation(d, s_, AF.Copy),
                      wait=[("VE", tq)], drain=True)
            tve = ve(lambda e, d=xdl, s_=xdq, h=xdh:
                     e.tensor_tensor(d, s_, h, OP.subtract),
                     wait=[("ACT", t_h)], drain=True)
            phase3.wtok[0] = tve
            for j in range(2):
                ob = 2 * obp + j
                for half, srcv, o_dst in (("h", xdh, o_wfh), ("l", xdl, o_wfl)):
                    for cc in range(8):
                        sl = cc % 2
                        psl = psb16(sl)
                        t_mm = mm(lambda e, ps=psl,
                                  xb=srcv[:, j * C + cc * 128: j * C + (cc + 1) * 128]:
                                  e.transpose(ps, xb, idenb.ap()),
                                  wait=[("VE", tve), ("ACT", tsl_free[sl])])
                        d = bf16v(o_dst + (cc * C + ob * 128) * 2, 128)
                        last = (j == 1 and half == "l" and cc == 7)
                        t2 = sca(lambda e, d=d, s_=psl: e.activation(d, s_, AF.Copy),
                                 wait=[("MM", t_mm)], drain=last)
                        tsl_free[sl] = t2
        prep_guard = [("LDW0", p.tok("LDW0")), ("LDW1", p.tok("LDW1")),
                      ("VE", p.tok("VE"))]
        first_at = True
        x3_free = {0: 0, 1: 0}
        at_free = 0
        y_free = {0: 0, 1: 0}
        f_free = 0
        out_free = 0

        def stage1(b, tt, half, t_at, gidx):
            apar = gidx % 2
            atbase_h = o_ath + apar * (NPAIR * 256 * 2)
            atbase_l = o_atl + apar * (NPAIR * 256 * 2)
            if not hasattr(stage1, "first"):
                stage1.first = True
            """Wout 3-pass + bias + residual for 2 token blocks -> sres[par]."""
            nonlocal x3_free
            par = gidx % 2
            tb0 = tt * (Tt // 128) + half * 2
            t0_tok = tb0 * 128
            x3 = f32(o_x3, 2 * C)
            x3v = x3.rearrange("p (a c) -> p a c", a=2)
            xin = X[b, t0_tok:t0_tok + 256, :].rearrange("(a q) c -> q a c", q=128)
            wx = [("VE", max(x3_free.values()))]
            if stage1.first:
                wx += prep_guard
                stage1.first = False
            t_x = dma("sync", f"LDX{par}", x3v, xin, wait=wx)
            sres = f32(sres_off(gidx % 3), 2 * C)
            for j in range(2):
                tb4 = half * 2 + j
                ypar = j % 2
                ypsum = PSU[ypar].ap()
                for ot in range(2):
                    yb = ypsum[:, ot * 512:(ot + 1) * 512]
                    ops = []
                    for cc in range(8):
                        ah = bf16v(atbase_h + (cc * 256 + j * 128) * 2, 128)
                        al = bf16v(atbase_l + (cc * 256 + j * 128) * 2, 128)
                        wh = bf16v(o_woh + (cc * C + ot * 512) * 2, 512)
                        wl = bf16v(o_wol + (cc * C + ot * 512) * 2, 512)
                        ops += [(ah, wh), (ah, wl), (al, wh)]
                    for i, (sa, ma) in enumerate(ops):
                        w0 = []
                        if i == 0:
                            w0 = [(f"LDA{apar}", t_at[0]), (f"LAG{apar}", t_at[1]),
                                  ("VE", y_free[ypar])]
                        mm(lambda e, pb=yb, sa=sa, ma=ma, st=(i == 0):
                           e.matmul(pb, sa, ma, start=st, stop=False), wait=w0)
                    mm(lambda e, pb=yb, ot=ot:
                       e.matmul(pb, ones1.ap(),
                                brow_h.ap()[:, 2 * C + ot * 512: 2 * C + (ot + 1) * 512],
                                start=False, stop=False))
                    t_mm = mm(lambda e, pb=yb, ot=ot:
                              e.matmul(pb, ones1.ap(),
                                       brow_l.ap()[:, 2 * C + ot * 512: 2 * C + (ot + 1) * 512],
                                       start=False, stop=True))
                tve = ve(lambda e, s_=sres[:, j * C:(j + 1) * C], pb=ypsum,
                         x3j=x3[:, j * C:(j + 1) * C]:
                         e.tensor_tensor(s_, pb, x3j, OP.add),
                         wait=[("MM", t_mm), (f"LDX{par}", t_x)],
                         drain=(j == 1))
                y_free[ypar] = tve
            x3_free[par] = tve
            return sres

        def stage2a(b, tt, half, sres, gidx):
            x1t = f32(o_x1t if gidx % 2 == 0 else o_x1t2, 2 * C)
            layernorm2([sres[:, 0:C], sres[:, C:2 * C]],
                       [x1t[:, 0:C], x1t[:, C:2 * C]], [0, 1])
            xdq = f32(o_qb, 2 * C)
            tq = quant_chain(x1t, xdq, 2 * C, gp_ladder=GP_LADDER)
            xdh = bf16v(o_xdh, 2 * C); xdl = bf16v(o_xdl, 2 * C)
            t_h = sca(lambda e, d=xdh, s_=xdq: e.activation(d, s_, AF.Copy),
                      wait=[("VE", tq)], drain=True)
            tve = ve(lambda e, d=xdl, s_=xdq, h=xdh:
                     e.tensor_tensor(d, s_, h, OP.subtract),
                     wait=[("ACT", t_h)], drain=True)
            return x1t, tve, t_h

        def stage2b(b, tt, half, sres, gidx, x1t, tve, t_h):
            tpar = gidx % 2
            o_xdth = o_xdtd + tpar * 2 * (2 * C * 2)
            o_xdtl = o_xdth + 2 * C * 2
            xdh = bf16v(o_xdh, 2 * C); xdl = bf16v(o_xdl, 2 * C)
            tsl = {0: 0, 1: 0}
            def psb16_t(sl):
                return PSU[3].ap().bitcast(BF16)[:, sl * 1024: sl * 1024 + 128]
            for j in range(2):
                for hN, srcv, o_dst in (("h", xdh, o_xdth), ("l", xdl, o_xdtl)):
                    for half4 in range(2):
                        sl = half4
                        for q4 in range(4):
                            cc = half4 * 4 + q4
                            psl = PSU[3].ap().bitcast(BF16)[
                                :, sl * 1024 + q4 * 128: sl * 1024 + (q4 + 1) * 128]
                            wt = []
                            if q4 == 0:
                                wt = ([("ACT", max(t_h, tsl[sl]))] if hN == "h" else
                                      [("VE", tve), ("ACT", tsl[sl])])
                            t_mm = mm(lambda e, ps=psl,
                                      xb=srcv[:, j * C + cc * 128:
                                              j * C + (cc + 1) * 128]:
                                      e.transpose(ps, xb, idenb.ap()),
                                      wait=wt)
                        d = bf16v(o_dst + (j * C + half4 * 512) * 2, 512)
                        pw = PSU[3].ap().bitcast(BF16)[:, sl * 1024: sl * 1024 + 512]
                        last = (j == 1 and hN == "l" and half4 == 1)
                        t2 = sca(lambda e, d=d, s_=pw: e.activation(d, s_, AF.Copy),
                                 wait=[("MM", t_mm)], drain=last)
                        tsl[sl] = t2
            return p.tok("ACT")

        def stage3(b, tt, half, sres, gidx, x1t, t_xdt):
            nonlocal f_free, out_free
            tpar = gidx % 2
            o_xdth = o_xdtd + tpar * 2 * (2 * C * 2)
            o_xdtl = o_xdth + 2 * C * 2
            tb0 = tt * (Tt // 128) + half * 2
            t0_tok = tb0 * 128
            o3 = f32(o_o3, 2 * C)
            for j in range(2):
                fpsum = PSU[2].ap()
                for ot in range(2):
                    fb = fpsum[:, ot * 512:(ot + 1) * 512]
                    ops = []
                    for cc in range(8):
                        xh = bf16v(o_xdth + (j * C + cc * 128) * 2, 128)
                        xl = bf16v(o_xdtl + (j * C + cc * 128) * 2, 128)
                        wh = bf16v(o_wfh + (cc * C + ot * 512) * 2, 512)
                        wl = bf16v(o_wfl + (cc * C + ot * 512) * 2, 512)
                        ops += [(xh, wh), (xh, wl), (xl, wh)]
                    for i, (sa, ma) in enumerate(ops):
                        w0 = []
                        if i == 0:
                            w0 = [("ACT", t_xdt), ("VE", f_free)]
                        mm(lambda e, pb=fb, sa=sa, ma=ma, st=(i == 0):
                           e.matmul(pb, sa, ma, start=st, stop=False), wait=w0)
                    mm(lambda e, pb=fb, ot=ot:
                       e.matmul(pb, ones1.ap(),
                                brow_h.ap()[:, C + ot * 512: C + (ot + 1) * 512],
                                start=False, stop=False))
                    t_mmf = mm(lambda e, pb=fb, ot=ot:
                               e.matmul(pb, ones1.ap(),
                                        brow_l.ap()[:, C + ot * 512: C + (ot + 1) * 512],
                                        start=False, stop=True))
                tve = ve(lambda e, s_=sres[:, j * C:(j + 1) * C], pb=fpsum,
                         x1j=x1t[:, j * C:(j + 1) * C]:
                         e.tensor_tensor(s_, pb, x1j, OP.add),
                         wait=[("MM", t_mmf)],
                         drain=True)
                f_free = tve
            layernorm2([sres[:, 0:C], sres[:, C:2 * C]],
                       [o3[:, 0:C], o3[:, C:2 * C]], [0, 1],
                       dummies=[o3[:, 0:C], o3[:, C:2 * C]],
                       wait_first=[("STO0", out_free)])
            tvo = p.tok("VE")
            o3v = o3.rearrange("p (a c) -> p a c", a=2)
            oout = OUTX[b, t0_tok:t0_tok + 256, :].rearrange("(a q) c -> q a c", q=128)
            out_free = dma("gpsimd", "STO0", oout, o3v, wait=[("VE", tvo)])

        pend_s2 = None
        pend_s3 = None
        groups = [(b, tt, half) for b in range(Bb) for tt in range(NTT)
                  for half in range(Tt // 256)]
        at_mm_tok = {0: 0, 1: 0}
        at_ld_tok = {}

        def load_at(gidx, first=False):
            gb, gtt, ghalf = groups[gidx]
            apar = gidx % 2
            t0_tok = gtt * Tt + ghalf * 256
            w0 = [("MM", at_mm_tok[apar])] + (prep_guard if first else [])
            dsth = bf16v(o_ath + apar * (NPAIR * 256 * 2),
                         NPAIR * 256).rearrange("q (a c) -> q a c", a=NPAIR)
            dstl = bf16v(o_atl + apar * (NPAIR * 256 * 2),
                         NPAIR * 256).rearrange("q (a c) -> q a c", a=NPAIR)
            th = dma("sync", f"LDA{apar}", dsth,
                     AT_H[gb, :, :, t0_tok:t0_tok + 256].rearrange("a q c -> q a c"),
                     wait=w0)
            tl = dma("gpsimd", f"LAG{apar}", dstl,
                     AT_L[gb, :, :, t0_tok:t0_tok + 256].rearrange("a q c -> q a c"),
                     wait=list(w0))
            at_ld_tok[gidx] = (th, tl)

        load_at(0, first=True)
        for gidx, (b, tt, half) in enumerate(groups):
            a_out = None
            if pend_s2 is not None:
                a_out = stage2a(*pend_s2)
            sres = stage1(b, tt, half, at_ld_tok[gidx], gidx)
            at_mm_tok[gidx % 2] = p.tok("MM")
            if gidx + 1 < len(groups):
                load_at(gidx + 1)
            if pend_s3 is not None:
                stage3(*pend_s3)
            if pend_s2 is not None:
                t_xdt = stage2b(*pend_s2, *a_out)
                pend_s3 = pend_s2 + (a_out[0], t_xdt)
            pend_s2 = (b, tt, half, sres, gidx)
        if pend_s2 is not None:
            a_out = stage2a(*pend_s2)
            if pend_s3 is not None:
                stage3(*pend_s3)
            t_xdt = stage2b(*pend_s2, *a_out)
            stage3(*(pend_s2 + (a_out[0], t_xdt)))
        p.emit("gpsimd", lambda e: e.nop(),
               wait=[("STO0", p.tok("STO0"))])
    phase3.wtok = {}

    for b in range(Bb):
        phase1(b)
        for g in range(NG):
            phase2(b, g)
    phase3()

    with nc.Block() as block:
        @block.sync
        def _(eng):
            p.replay("sync", eng)
        @block.tensor
        def _(eng):
            p.replay("tensor", eng)
        @block.vector
        def _(eng):
            p.replay("vector", eng)
        @block.scalar
        def _(eng):
            p.replay("scalar", eng)
        @block.gpsimd
        def _(eng):
            p.replay("gpsimd", eng)
    mybir.codegen_inst_isa_subclasses(nc)
    nc._ctx.close()
    return nc




_T_FULL = 2048

def kernel(L0_attn_x, L0_attn_Wqkv, L0_attn_bqkv, L0_attn_Wout, L0_attn_bout,
           L0_ffn_weight, L0_ffn_bias):
    from concourse.bass_utils import run_bass_kernel_spmd
    x = np.ascontiguousarray(np.asarray(L0_attn_x, dtype=np.float32))
    Wqkv = np.ascontiguousarray(np.asarray(L0_attn_Wqkv, dtype=np.float32))
    bqkv = np.ascontiguousarray(np.asarray(L0_attn_bqkv, dtype=np.float32))
    Wout = np.ascontiguousarray(np.asarray(L0_attn_Wout, dtype=np.float32))
    bout = np.ascontiguousarray(np.asarray(L0_attn_bout, dtype=np.float32))
    Wffn = np.ascontiguousarray(np.asarray(L0_ffn_weight, dtype=np.float32))
    bffn = np.ascontiguousarray(np.asarray(L0_ffn_bias, dtype=np.float32))
    B_full, T, Cc = x.shape
    assert (B_full, T, Cc) == (16, _T_FULL, 1024), (x.shape,)
    n_cores = 8
    nc = build_core_program(T=T, Bb=B_full // n_cores)
    in_maps = []
    for c in range(n_cores):
        xs = x[c * 2:(c + 1) * 2]
        in_maps.append(host_inputs_for_core(xs, Wqkv, bqkv, Wout, bout,
                                            Wffn, bffn, T))
    res = run_bass_kernel_spmd(nc, in_maps, core_ids=list(range(n_cores)))
    out = np.concatenate([np.asarray(r["out"], dtype=np.float32)
                          for r in res.results], axis=0)
    return out
